# version 10
# baseline (speedup 1.0000x reference)
import numpy as np
import ml_dtypes
import bass_rust
import concourse.bass as bass
import concourse.mybir as mybir
from concourse.bass_core_rust import IndirectOffsetOnAxis as IOA

F32 = mybir.dt.float32
BF16 = mybir.dt.bfloat16
I32 = mybir.dt.int32
U32 = mybir.dt.uint32
AF = mybir.ActivationFunctionType
ALU = mybir.AluOpType
AX = mybir.AxisListType.X

S = 4096
D = 1024
NTC = S // 128
NDC = D // 128


class Sem:
    def __init__(self, h):
        self.h = h
        self.n = 0


class Prog:
    ENGS = ("tensor", "vector", "scalar", "gpsimd", "sync")

    def __init__(self, nc):
        self.nc = nc
        self.q = {e: [] for e in self.ENGS}

    def do(self, eng, fn, inc=None, inc2=None):
        if inc is not None:
            inc[0].n += inc[1]
        if inc2 is not None:
            inc2[0].n += inc2[1]

        def emit(e, fn=fn, inc=inc, inc2=inc2):
            inst = fn(e)
            if inc is not None:
                inst.then_inc(inc[0].h, inc[1])
            if inc2 is not None:
                inst.then_inc(inc2[0].h, inc2[1])
        self.q[eng].append(emit)
        return inc[0].n if inc is not None else None

    def drain(self, eng):
        self.q[eng].append(lambda e: e.drain())

    def wait(self, eng, sem, val=None):
        v = sem.n if val is None else val
        assert v >= 0
        self.q[eng].append(lambda e, s=sem.h, v=v: e.wait_ge(s, v))

    def finalize(self, block):
        for name in self.ENGS:
            fns = self.q[name]
            getattr(block, name)(lambda eng, fns=fns: [f(eng) for f in fns])


def new_sem(nc, stk, name):
    return Sem(stk.enter_context(nc.semaphore(name)))


def emit_ln(nc, P, pre, x_dram, xn_dram, tstk, sstk):
    xt = [tstk.enter_context(nc.sbuf_tensor(f"{pre}xt{i}", [128, D], F32)) for i in range(2)]
    xnb = [tstk.enter_context(nc.sbuf_tensor(f"{pre}xnb{i}", [128, D], BF16)) for i in range(2)]
    junk = tstk.enter_context(nc.sbuf_tensor(f"{pre}junk", [128, D], F32))
    st = tstk.enter_context(nc.sbuf_tensor(f"{pre}st", [128, 12], F32))
    s_ld = [new_sem(nc, sstk, pre + f"ld{i}") for i in range(2)]
    s_st = [new_sem(nc, sstk, pre + f"st{i}") for i in range(2)]
    s_sq = new_sem(nc, sstk, pre + "sq")
    s_rs = new_sem(nc, sstk, pre + "rs")
    s_v = new_sem(nc, sstk, pre + "v")
    s_fr = new_sem(nc, sstk, pre + "fr")

    P.do("vector", lambda e: e.memset(st[:, 8:9], 1e-5), inc=(s_v, 1))
    for i in range(NTC):
        b = i % 2
        if i >= 2:
            P.wait("sync", s_fr, i - 1)
        P.do("sync", lambda e, b=b, i=i: e.dma_start(xt[b][:], x_dram[i * 128:(i + 1) * 128, :]),
             inc=(s_ld[b], 16))
        ld_i = s_ld[b].n
        P.wait("scalar", s_ld[b], ld_i)
        P.do("scalar", lambda e, b=b: e.activation(junk[:], xt[b][:], AF.Square,
                                                   accum_out=st[:, 2:3]), inc=(s_sq, 1))
        P.drain("scalar")
        sq_i = s_sq.n
        P.wait("vector", s_ld[b], ld_i)
        P.do("vector", lambda e, b=b: e.tensor_reduce(st[:, 0:1], xt[b][:], axis=AX, op=ALU.add))
        P.drain("vector")
        P.do("vector", lambda e: e.tensor_scalar(st[:, 1:2], st[:, 0:1], -1.0 / D, None,
                                                 op0=ALU.mult))
        P.drain("vector")
        P.wait("vector", s_sq, sq_i)
        P.do("vector", lambda e: e.tensor_scalar(st[:, 3:4], st[:, 2:3], 1.0 / D, None,
                                                 op0=ALU.mult))
        P.do("vector", lambda e: e.tensor_tensor(st[:, 4:5], st[:, 1:2], st[:, 1:2], op=ALU.mult))
        P.drain("vector")
        P.do("vector", lambda e: e.tensor_tensor(st[:, 5:6], st[:, 3:4], st[:, 4:5],
                                                 op=ALU.subtract), inc=(s_v, 1))
        var_i = s_v.n
        P.wait("scalar", s_v, var_i)
        P.do("scalar", lambda e: e.activation(st[:, 6:7], st[:, 5:6], AF.Ln, bias=st[:, 8:9]))
        P.drain("scalar")
        P.do("scalar", lambda e: e.activation(st[:, 7:8], st[:, 6:7], AF.Exp, scale=-0.5),
             inc=(s_rs, 1))
        rs_i = s_rs.n
        P.wait("vector", s_rs, rs_i)
        if i >= 2:
            P.wait("vector", s_st[b], ((i - 2) // 2 + 1) * 16)
        P.do("vector", lambda e, b=b: e.tensor_scalar(xnb[b][:], xt[b][:], st[:, 1:2], st[:, 7:8],
                                                      op0=ALU.add, op1=ALU.mult), inc=(s_fr, 1))
        fr_i = s_fr.n
        P.wait("sync", s_fr, fr_i)
        P.do("sync", lambda e, b=b, i=i: e.dma_start(xn_dram[i * 128:(i + 1) * 128, :], xnb[b][:]),
             inc=(s_st[b], 16))
    return s_st


def wait_all(P, eng, sems):
    for sm in sems:
        P.wait(eng, sm, sm.n)


def emit_xnT(nc, P, pre, xn_dram, stk, s_after):
    xnT = stk.enter_context(nc.sbuf_tensor(pre + "xnT", [128, NDC * S], BF16))
    s_tp = new_sem(nc, stk, pre + "tp")
    wait_all(P, "scalar", s_after)
    for dc in range(NDC):
        P.do("scalar", lambda e, dc=dc: e.dma_start(
            xnT[:, dc * S:(dc + 1) * S], xn_dram[:, dc * 128:(dc + 1) * 128], transpose=True),
            inc=(s_tp, 16))
    return xnT, s_tp


def build_l1(nc, P, stk, x1_dram, out_dram, w, pre="B_", do_ffn=True):
    from contextlib import ExitStack
    E1 = 8
    H1 = 4096
    xn_dram = nc.dram_tensor(pre + "xn", [S, D], BF16, kind="Internal").ap()

    with ExitStack() as ln_stk:
        s_xn = emit_ln(nc, P, pre + "ln", x1_dram, xn_dram, ln_stk, stk)
    xnT, s_tp = emit_xnT(nc, P, pre, xn_dram, stk, s_xn)
    tp_all = s_tp.n

    rw_bf = stk.enter_context(nc.sbuf_tensor(pre + "rw", [128, NDC * 8], BF16))
    W8 = stk.enter_context(nc.sbuf_tensor(pre + "W8", [128, NTC * 8], F32))
    rsc = stk.enter_context(nc.sbuf_tensor(pre + "rsc", [128, 40], F32))
    s_rw = new_sem(nc, stk, pre + "rw")
    s_rmm = new_sem(nc, stk, pre + "rmm")
    s_rv = new_sem(nc, stk, pre + "rv")
    s_rs2 = new_sem(nc, stk, pre + "rs2")
    s_w8 = new_sem(nc, stk, pre + "w8")

    P.do("gpsimd", lambda e: e.dma_start(
        rw_bf[:, 0:NDC * 8].rearrange("p (dc e) -> p dc e", dc=NDC),
        w["rw"].rearrange("(dc p) e -> p dc e", p=128)), inc=(s_rw, 16))

    with nc.psum_tensor(pre + "psr0", [128, 512], F32) as psr0, \
         nc.psum_tensor(pre + "psr1", [128, 512], F32) as psr1:
        ps_r = [psr0, psr1]
        P.wait("tensor", s_rw, s_rw.n)
        P.wait("tensor", s_tp, tp_all)
        for i in range(NTC):
            b = i % 2
            if i >= 2:
                P.wait("tensor", s_rs2, i - 1)
            for dc in range(NDC):
                P.do("tensor", lambda e, b=b, i=i, dc=dc: e.matmul(
                    ps_r[b][:, 0:8], xnT[:, dc * S + i * 128: dc * S + (i + 1) * 128],
                    rw_bf[:, dc * 8:(dc + 1) * 8], start=(dc == 0), stop=(dc == NDC - 1)),
                    inc=(s_rmm, 1) if dc == NDC - 1 else None)
            mm_i = s_rmm.n
            P.wait("vector", s_rmm, mm_i)
            P.do("vector", lambda e, b=b: e.tensor_reduce(rsc[:, 29:30], ps_r[b][:, 0:8],
                                                          axis=AX, op=ALU.max))
            P.drain("vector")
            P.do("vector", lambda e: e.tensor_scalar(rsc[:, 24:25], rsc[:, 29:30], -1.0, None,
                                                     op0=ALU.mult), inc=(s_rv, 1))
            P.wait("scalar", s_rv, s_rv.n)
            P.do("scalar", lambda e, b=b: e.activation(rsc[:, 0:8], ps_r[b][:, 0:8], AF.Exp,
                                                       bias=rsc[:, 24:25]), inc=(s_rs2, 1))
            P.wait("vector", s_rs2, s_rs2.n)
            P.do("vector", lambda e: e.max(out=rsc[:, 8:16], in_=rsc[:, 0:8]))
            P.drain("vector")
            P.do("vector", lambda e: e.tensor_scalar(rsc[:, 16:24], rsc[:, 0:8], rsc[:, 11:12],
                                                     None, op0=ALU.is_ge))
            P.drain("vector")
            P.do("vector", lambda e: e.tensor_tensor(rsc[:, 0:8], rsc[:, 0:8], rsc[:, 16:24],
                                                     op=ALU.mult))
            P.drain("vector")
            P.do("vector", lambda e: e.tensor_reduce(rsc[:, 27:28], rsc[:, 0:8], axis=AX,
                                                     op=ALU.add))
            P.drain("vector")
            P.do("vector", lambda e: e.reciprocal(rsc[:, 28:29], rsc[:, 27:28]))
            P.drain("vector")
            P.do("vector", lambda e, i=i: e.tensor_scalar(W8[:, i * 8:(i + 1) * 8], rsc[:, 0:8],
                                                          rsc[:, 28:29], None, op0=ALU.mult),
                 inc=(s_w8, 1))
    w8_all = s_w8.n

    s_wld = new_sem(nc, stk, pre + "wld")
    s_umm = new_sem(nc, stk, pre + "umm")
    s_uev = new_sem(nc, stk, pre + "uev")
    s_scan = new_sem(nc, stk, pre + "scan")
    s_hbf = new_sem(nc, stk, pre + "hbf")
    s_x1 = [new_sem(nc, stk, pre + f"x1_{i}") for i in range(2)]
    s_smm = new_sem(nc, stk, pre + "smm")
    s_seedv = new_sem(nc, stk, pre + "seedv")
    s_seedw = [new_sem(nc, stk, pre + f"seedw{i}") for i in range(2)]

    with nc.sbuf_tensor(pre + "BT", [128, NDC * D], BF16) as BT_sb, \
         nc.sbuf_tensor(pre + "u", [128, S], F32) as u_sb, \
         nc.sbuf_tensor(pre + "a", [128, NDC], F32) as a_sb, \
         nc.sbuf_tensor(pre + "hbf", [128, NDC * S], BF16) as hbf, \
         nc.sbuf_tensor(pre + "WT", [128, NDC * D], BF16) as WT_bf, \
         nc.sbuf_tensor(pre + "bv", [1, D], BF16) as bvec, \
         nc.sbuf_tensor(pre + "on1", [1, 128], BF16) as ones1, \
         nc.sbuf_tensor(pre + "x1t0", [128, D], F32) as x1t0, \
         nc.sbuf_tensor(pre + "x1t1", [128, D], F32) as x1t1, \
         nc.sbuf_tensor(pre + "seed0", [128, D], F32) as seed0, \
         nc.sbuf_tensor(pre + "seed1", [128, D], F32) as seed1:
        x1t = [x1t0, x1t1]
        seed = [seed0, seed1]

        P.do("vector", lambda e: e.memset(ones1[:], 1.0), inc=(s_hbf, 1))
        wait_all(P, "gpsimd", s_xn)
        for dc in range(NDC):
            P.do("gpsimd", lambda e, dc=dc: e.dma_start(
                BT_sb[:, dc * D:(dc + 1) * D], w["BT"][dc * 128:(dc + 1) * 128, :]),
                inc=(s_wld, 16))
            P.do("gpsimd", lambda e, dc=dc: e.dma_start(
                WT_bf[:, dc * D:(dc + 1) * D], w["WT"][dc * 128:(dc + 1) * 128, :]),
                inc=(s_wld, 16))
        P.do("gpsimd", lambda e: e.dma_start(bvec[:], w["b"][None, :]), inc=(s_wld, 16))
        P.do("gpsimd", lambda e: e.dma_start(a_sb[:, 0:NDC], w["A2"]), inc=(s_wld, 16))
        wld_all = s_wld.n

        with nc.psum_tensor(pre + "psu0", [128, 512], F32) as psu0, \
             nc.psum_tensor(pre + "psu1", [128, 512], F32) as psu1:
            ps_u = [psu0, psu1]
            P.wait("tensor", s_wld, wld_all)
            P.wait("tensor", s_rs2, NTC)
            P.wait("vector", s_wld, wld_all)
            for nch in range(NDC):
                for t5 in range(8):
                    b = t5 % 2
                    if nch * 8 + t5 >= 2:
                        P.wait("tensor", s_uev, nch * 8 + t5 - 1)
                    for dc in range(NDC):
                        P.do("tensor", lambda e, b=b, nch=nch, t5=t5, dc=dc: e.matmul(
                            ps_u[b][:], BT_sb[:, dc * D + nch * 128: dc * D + (nch + 1) * 128],
                            xnT[:, dc * S + t5 * 512: dc * S + (t5 + 1) * 512],
                            start=(dc == 0), stop=(dc == NDC - 1)),
                            inc=(s_umm, 1) if dc == NDC - 1 else None)
                    P.wait("scalar", s_umm, s_umm.n)
                    P.do("scalar", lambda e, b=b, t5=t5: e.activation(
                        u_sb[:, t5 * 512:(t5 + 1) * 512], ps_u[b][:], AF.Copy), inc=(s_uev, 1))
                P.wait("vector", s_uev, s_uev.n)
                P.do("vector", lambda e, nch=nch: e.tensor_tensor_scan(
                    hbf[:, nch * S:(nch + 1) * S],
                    a_sb[:, nch:nch + 1].to_broadcast([128, S]), u_sb[:], 0.0,
                    op0=ALU.mult, op1=ALU.add), inc=(s_scan, 1))
                P.wait("scalar", s_scan, s_scan.n)
        scan_all = s_scan.n

        with nc.psum_tensor(pre + "pss0", [128, D], F32) as pss0, \
             nc.psum_tensor(pre + "pss1", [128, D], F32) as pss1:
            ps_s = [pss0, pss1]
            P.wait("tensor", s_hbf, 1)
            P.wait("tensor", s_scan, scan_all)
            for i in range(NTC):
                b = i % 2
                if i >= 2:
                    P.wait("sync", s_seedv, 2 * (i - 1))
                P.do("sync", lambda e, b=b, i=i: e.dma_start(
                    x1t[b][:], x1_dram[i * 128:(i + 1) * 128, :]), inc=(s_x1[b], 16))
                x1_i = s_x1[b].n
                if i >= 2:
                    P.wait("tensor", s_seedv, 2 * (i - 1))
                for half in range(2):
                    for nch in range(NDC):
                        P.do("tensor", lambda e, b=b, i=i, half=half, nch=nch: e.matmul(
                            ps_s[b][:, half * 512:(half + 1) * 512],
                            hbf[:, nch * S + i * 128: nch * S + (i + 1) * 128],
                            WT_bf[:, nch * D + half * 512: nch * D + (half + 1) * 512],
                            start=(nch == 0), stop=False))
                    if True:
                        P.do("tensor", lambda e, b=b, half=half: e.matmul(
                            ps_s[b][:, half * 512:(half + 1) * 512], ones1[:],
                            bvec[:, half * 512:(half + 1) * 512], start=False, stop=True),
                            inc=(s_smm, 1))
                    else:
                        P.do("tensor", lambda e, b=b, half=half: e.matmul(
                            ps_s[b][:, half * 512:(half + 1) * 512], ones1[:],
                            bvec[:, half * 512:(half + 1) * 512], start=False, stop=True),
                            inc=(s_smm, 1))
                mm_i = s_smm.n
                P.wait("vector", s_smm, mm_i)
                P.wait("vector", s_x1[b], x1_i)
                if i >= 2:
                    P.wait("vector", s_seedw[b], ((i - 2) // 2 + 1) * 16)
                P.do("vector", lambda e, b=b: e.tensor_tensor(
                    seed[b][:, 0:512], ps_s[b][:, 0:512], x1t[b][:, 0:512], op=ALU.add),
                    inc=(s_seedv, 1))
                P.do("vector", lambda e, b=b: e.tensor_tensor(
                    seed[b][:, 512:1024], ps_s[b][:, 512:1024], x1t[b][:, 512:1024], op=ALU.add),
                    inc=(s_seedv, 1))
                P.wait("gpsimd", s_seedv, s_seedv.n)
                P.do("gpsimd", lambda e, b=b, i=i: e.dma_start(
                    out_dram[i * 128:(i + 1) * 128, :], seed[b][:]), inc=(s_seedw[b], 16))
        smm_all = s_smm.n

    if not do_ffn:
        wait_all(P, "gpsimd", s_seedw)
        return s_seedw
    TT = 512
    NTT = S // TT
    FQ = 1024
    NQ = H1 // FQ
    NFC = FQ // 128
    up_sb = [stk.enter_context(nc.sbuf_tensor(f"{pre}up{i}", [128, NDC * FQ], BF16))
             for i in range(2)]
    dn_sb = [stk.enter_context(nc.sbuf_tensor(f"{pre}dn{i}", [128, NFC * D], BF16))
             for i in range(2)]
    hT = stk.enter_context(nc.sbuf_tensor(pre + "hT", [128, NFC * TT], BF16))
    stage = [stk.enter_context(nc.sbuf_tensor(f"{pre}stg{i}", [128, 4 * D], F32))
             for i in range(2)]
    s_wq = [new_sem(nc, stk, pre + f"wq{i}") for i in range(2)]
    s_qfree = new_sem(nc, stk, pre + "qfree")
    s_hmm = new_sem(nc, stk, pre + "hmm")
    s_gel = new_sem(nc, stk, pre + "gel")
    s_omm = new_sem(nc, stk, pre + "omm")
    s_stg = new_sem(nc, stk, pre + "stg")
    s_acc = [new_sem(nc, stk, pre + f"acc{i}") for i in range(2)]

    def load_quarter(qi, e_, q_):
        buf = qi % 2
        if qi >= 2:
            P.wait("gpsimd", s_qfree, qi - 1)
        P.do("gpsimd", lambda e, e_=e_, q_=q_, buf=buf: e.dma_start(
            up_sb[buf][:].rearrange("p (dc f) -> p dc f", dc=NDC),
            w["up"][e_].rearrange("(dc p) f -> p dc f", p=128)[:, :, q_ * FQ:(q_ + 1) * FQ]),
            inc=(s_wq[buf], 16))
        P.do("gpsimd", lambda e, e_=e_, q_=q_, buf=buf: e.dma_start(
            dn_sb[buf][:].rearrange("p (fc d) -> p fc d", fc=NFC),
            w["dn"][e_][q_ * FQ:(q_ + 1) * FQ, :].rearrange("(fc p) d -> p fc d", p=128)),
            inc=(s_wq[buf], 16))

    quarters = [(e_, q_) for e_ in range(E1) for q_ in range(NQ)]
    P.wait("scalar", s_w8, w8_all)
    wait_all(P, "gpsimd", s_seedw)
    P.wait("gpsimd", s_smm, smm_all)
    load_quarter(0, *quarters[0])
    load_quarter(1, *quarters[1])
    with nc.psum_tensor(pre + "psh0", [128, TT], F32) as psh0, \
         nc.psum_tensor(pre + "psh1", [128, TT], F32) as psh1, \
         nc.psum_tensor(pre + "pso0", [128, D], F32) as pso0, \
         nc.psum_tensor(pre + "pso1", [128, D], F32) as pso1:
        ps_h = [psh0, psh1]
        ps_o = [pso0, pso1]
        for qi, (e_, q_) in enumerate(quarters):
            buf = qi % 2
            P.wait("tensor", s_wq[buf], (qi // 2 + 1) * 32)
            for tt in range(NTT):
                gi = qi * NTT + tt
                for fc in range(NFC):
                    hb = fc % 2
                    gidx = gi * NFC + fc
                    if gidx >= 2:
                        P.wait("tensor", s_gel, gidx - 1)
                    for dc in range(NDC):
                        P.do("tensor", lambda e, buf=buf, fc=fc, dc=dc, tt=tt, hb=hb: e.matmul(
                            ps_h[hb][:],
                            up_sb[buf][:, dc * FQ + fc * 128: dc * FQ + (fc + 1) * 128],
                            xnT[:, dc * S + tt * TT: dc * S + (tt + 1) * TT],
                            start=(dc == 0), stop=(dc == NDC - 1)),
                            inc=(s_hmm, 1) if dc == NDC - 1 else None)
                    P.wait("scalar", s_hmm, s_hmm.n)
                    if gi >= 1 and fc == 0:
                        P.wait("scalar", s_omm, gi * 4)
                    P.do("scalar", lambda e, fc=fc, hb=hb: e.activation(
                        hT[:, fc * TT:(fc + 1) * TT], ps_h[hb][:], AF.Gelu_apprx_tanh),
                        inc=(s_gel, 1))
                gel_i = s_gel.n
                P.wait("tensor", s_gel, gel_i)
                for tc4 in range(4):
                    ob = tc4 % 2
                    oidx = gi * 4 + tc4
                    if oidx >= 2:
                        P.wait("tensor", s_stg, oidx - 1)
                    for fc in range(NFC):
                        for half in range(2):
                            P.do("tensor",
                                 lambda e, buf=buf, fc=fc, tc4=tc4, half=half, ob=ob: e.matmul(
                                     ps_o[ob][:, half * 512:(half + 1) * 512],
                                     hT[:, fc * TT + tc4 * 128: fc * TT + (tc4 + 1) * 128],
                                     dn_sb[buf][:, fc * D + half * 512: fc * D + (half + 1) * 512],
                                     start=(fc == 0), stop=(fc == NFC - 1)),
                                 inc=(s_omm, 1) if (fc == NFC - 1 and half == 1) else None)
                    omm_i = s_omm.n
                    P.wait("scalar", s_omm, omm_i)
                    sb = gi % 2
                    if gi >= 2 and tc4 == 0:
                        P.wait("scalar", s_acc[sb], ((gi - 2) // 2 + 1) * 16)
                    P.do("scalar", lambda e, tc4=tc4, ob=ob, tt=tt, e_=e_, sb=sb: e.activation(
                        stage[sb][:, tc4 * D:(tc4 + 1) * D], ps_o[ob][:], AF.Copy,
                        scale=W8[:, (tt * 4 + tc4) * 8 + e_: (tt * 4 + tc4) * 8 + e_ + 1]),
                        inc=(s_stg, 1))
                stg_i = s_stg.n
                P.wait("gpsimd", s_stg, stg_i)
                P.do("gpsimd", lambda e, tt=tt, sb=(gi % 2): e.dma_start(
                    out_dram[tt * TT:(tt + 1) * TT, :].rearrange("(c p) d -> p c d", p=128),
                    stage[sb][:].rearrange("p (c d) -> p c d", c=4), accum_op=ALU.add),
                    inc=(s_acc[gi % 2], 16))
            P.do("tensor", lambda e: e.nop(), inc=(s_qfree, 1))
            if qi + 2 < len(quarters):
                load_quarter(qi + 2, *quarters[qi + 2])

    wait_all(P, "gpsimd", s_acc)
    return s_acc



NH = 16
DK = 64
E0 = 16
CAP = 256
SINK_IT = 6
NQT = S // 512
NKC = S // 128


def host_bucket(d):
    import math
    if d > 0:
        ret, ad = 16, d
    else:
        ret, ad = 0, -d
    if ad < 8:
        return ret + ad
    v = 8 + int(math.log(ad / 8) / math.log(128 / 8) * 8)
    return ret + min(v, 15)


def host_consts():
    oh = np.zeros((32, 8191), np.float32)
    for e in range(8191):
        oh[host_bucket(4095 - e), e] = 1.0
    J = np.zeros((128, 128), np.float32)
    for p in range(128):
        J[p, 127 - p] = 1.0
    ident = np.eye(128, dtype=np.float32)
    return {
        "oneh": oh.astype(ml_dtypes.bfloat16),
        "J": J.astype(ml_dtypes.bfloat16),
        "identf": ident,
        "identb": ident.astype(ml_dtypes.bfloat16),
    }


def band_kind(qt, kc):
    if 128 * kc + 127 - 512 * qt <= -91:
        return 0
    if 128 * kc - (512 * qt + 511) >= 91:
        return 1
    return 2


def build_l0(nc, P, stk, x_dram, x1_dram, w, pre="A_", stage="full", dump_cb=None):
    from contextlib import ExitStack
    l0t = ExitStack()
    xn_dram = nc.dram_tensor(pre + "xn", [S, D], BF16, kind="Internal").ap()
    wbias_dram = nc.dram_tensor(pre + "wb", [NH, 8191], BF16, kind="Internal").ap()

    sc = l0t.enter_context(nc.sbuf_tensor(pre + "sc", [128, 32], F32))
    sc2 = l0t.enter_context(nc.sbuf_tensor(pre + "sc2", [128, 32], F32))
    cb = l0t.enter_context(nc.sbuf_tensor(pre + "cb", [128, 3 * NH], F32))
    onesf = l0t.enter_context(nc.sbuf_tensor(pre + "onesf", [128, 128], F32))
    ones_bf = l0t.enter_context(nc.sbuf_tensor(pre + "onesb", [128, 128], BF16))
    J_sb = l0t.enter_context(nc.sbuf_tensor(pre + "J", [128, 128], BF16))
    idf_sb = l0t.enter_context(nc.sbuf_tensor(pre + "idf", [128, 128], F32))
    idb_sb = l0t.enter_context(nc.sbuf_tensor(pre + "idb", [128, 128], BF16))
    trow_l = l0t.enter_context(nc.sbuf_tensor(pre + "trl", [1, 16], F32))
    trow_r = l0t.enter_context(nc.sbuf_tensor(pre + "trr", [1, 16], F32))
    tabf = l0t.enter_context(nc.sbuf_tensor(pre + "tabf", [1, 512], F32))
    G_bf = l0t.enter_context(nc.sbuf_tensor(pre + "G", [16, S], BF16))
    gatesT = l0t.enter_context(nc.sbuf_tensor(pre + "gatesT", [128, 2 * E0], F32))
    idxT = l0t.enter_context(nc.sbuf_tensor(pre + "idxT", [128, 2 * E0], I32))
    s_cst = new_sem(nc, stk, pre + "cst")
    s_ones = new_sem(nc, stk, pre + "ones")

    P.do("gpsimd", lambda e: e.dma_start(trow_l[:], w["table"][15:16, :]), inc=(s_cst, 16))
    P.do("gpsimd", lambda e: e.dma_start(trow_r[:], w["table"][31:32, :]), inc=(s_cst, 16))
    P.do("gpsimd", lambda e: e.dma_start(tabf[:], w["tabflat"][None, :]), inc=(s_cst, 16))
    P.do("gpsimd", lambda e: e.dma_start(J_sb[:], w["J"]), inc=(s_cst, 16))
    P.do("gpsimd", lambda e: e.dma_start(idf_sb[:], w["identf"]), inc=(s_cst, 16))
    P.do("gpsimd", lambda e: e.dma_start(idb_sb[:], w["identb"]), inc=(s_cst, 16))
    cst_all = s_cst.n
    P.do("vector", lambda e: e.memset(onesf[:], 1.0))
    P.do("vector", lambda e: e.memset(ones_bf[:], 1.0), inc=(s_ones, 1))
    P.wait("vector", s_cst, cst_all)
    P.do("vector", lambda e: e.tensor_reduce(sc[0:1, 31:32], tabf[:], axis=AX, op=ALU.max,
                                             apply_absolute_value=True), inc=(s_ones, 1))

    with ExitStack() as ln_stk:
        s_xn = emit_ln(nc, P, pre + "ln", x_dram, xn_dram, ln_stk, stk)
    mid_stk = l0t.enter_context(ExitStack())
    attnT = mid_stk.enter_context(nc.sbuf_tensor(pre + "attnT", [128, NDC * S], BF16))
    xnT_stk = mid_stk.enter_context(ExitStack())
    xnT, s_tp = emit_xnT(nc, P, pre, xn_dram, xnT_stk, s_xn)
    tp_all = s_tp.n

    s_wbm = new_sem(nc, stk, pre + "wbm")
    s_wbe = new_sem(nc, stk, pre + "wbe")
    s_wbw = new_sem(nc, stk, pre + "wbw")
    with nc.sbuf_tensor(pre + "tabb", [32, 16], BF16) as tab_bf, \
         nc.sbuf_tensor(pre + "oneh", [32, 8191], BF16) as oneh_sb, \
         nc.sbuf_tensor(pre + "wrow", [16, 8191], BF16) as wrow, \
         nc.psum_tensor(pre + "psw0", [128, 512], F32) as psw0, \
         nc.psum_tensor(pre + "psw1", [128, 512], F32) as psw1:
        psw = [psw0, psw1]
        s_wbl = new_sem(nc, stk, pre + "wbl")
        wait_all(P, "gpsimd", s_xn)
        P.do("gpsimd", lambda e: e.dma_start(tab_bf[:], w["table"]), inc=(s_wbl, 16))
        P.do("gpsimd", lambda e: e.dma_start(oneh_sb[:], w["oneh"]), inc=(s_wbl, 16))
        P.wait("tensor", s_wbl, s_wbl.n)
        for t in range(16):
            n = 511 if t == 15 else 512
            pb = t % 2
            if t >= 2:
                P.wait("tensor", s_wbe, t - 1)
            P.do("tensor", lambda e, t=t, n=n, pb=pb: e.matmul(
                psw[pb][0:16, 0:n], tab_bf[:], oneh_sb[:, t * 512:t * 512 + n],
                start=True, stop=True), inc=(s_wbm, 1))
            P.wait("scalar", s_wbm, s_wbm.n)
            P.do("scalar", lambda e, t=t, n=n, pb=pb: e.activation(
                wrow[:, t * 512:t * 512 + n], psw[pb][0:16, 0:n], AF.Copy), inc=(s_wbe, 1))
        P.wait("sync", s_wbe, s_wbe.n)
        P.do("sync", lambda e: e.dma_start(wbias_dram, wrow[:]), inc=(s_wbw, 16))
    wbw_all = s_wbw.n



    s_gql = new_sem(nc, stk, pre + "gql")
    s_gmm = new_sem(nc, stk, pre + "gmm")
    s_gev = new_sem(nc, stk, pre + "gev")
    s_sks = new_sem(nc, stk, pre + "sks")
    s_skv = new_sem(nc, stk, pre + "skv")
    s_skt = new_sem(nc, stk, pre + "skt")
    s_tkv = new_sem(nc, stk, pre + "tkv")
    s_gdone = new_sem(nc, stk, pre + "gdone")

    with nc.sbuf_tensor(pre + "gqT", [128, NDC * E0], BF16) as gqT_bf, \
         nc.sbuf_tensor(pre + "tt", [128, NTC * E0], F32) as tt_sb, \
         nc.sbuf_tensor(pre + "ex", [128, NTC * E0], F32) as ex_sb, \
         nc.sbuf_tensor(pre + "lsn", [128, NTC], F32) as lsn_sb, \
         nc.sbuf_tensor(pre + "aff", [16, S], F32) as aff_sb, \
         nc.sbuf_tensor(pre + "affw", [16, S], F32) as affw_sb, \
         nc.sbuf_tensor(pre + "m8g", [16, 8], F32) as m8g, \
         nc.sbuf_tensor(pre + "gates", [16, CAP], F32) as gates_sb, \
         nc.sbuf_tensor(pre + "gidx", [16, CAP], U32) as gidx_sb, \
         nc.sbuf_tensor(pre + "lt", [1, 2], F32) as lt_sb, \
         nc.psum_tensor(pre + "psg0", [128, 512], F32) as psg0, \
         nc.psum_tensor(pre + "psg1", [128, 512], F32) as psg1:
        ps_g = [psg0, psg1]
        P.wait("gpsimd", s_wbw, wbw_all)
        P.do("gpsimd", lambda e: e.dma_start(
            gqT_bf[:].rearrange("p (dc e) -> p dc e", dc=NDC),
            w["gqT"].rearrange("(dc p) e -> p dc e", p=128)), inc=(s_gql, 16))
        P.do("gpsimd", lambda e: e.dma_start(lt_sb[:, 0:1], w["ltemp"][None, :]), inc=(s_gql, 16))
        gq_i = s_gql.n
        P.wait("scalar", s_gql, gq_i)
        P.do("scalar", lambda e: e.activation(lt_sb[:, 1:2], lt_sb[:, 0:1], AF.Exp, scale=-1.0),
             inc=(s_sks, 1))
        P.wait("tensor", s_sks, s_sks.n)
        P.wait("tensor", s_ones, 1)
        P.do("tensor", lambda e: e.matmul(psg0[0:128, 0:1], onesf[0:1, 0:128], lt_sb[0:1, 1:2],
                                          start=True, stop=True), inc=(s_skt, 1))
        P.wait("scalar", s_skt, s_skt.n)
        P.do("scalar", lambda e: e.activation(sc2[:, 0:1], psg0[0:128, 0:1], AF.Copy),
             inc=(s_sks, 1))
        itp_ready = s_sks.n

        P.wait("tensor", s_tp, tp_all)
        P.wait("tensor", s_gql, gq_i)
        P.wait("tensor", s_sks, itp_ready)
        for c in range(NTC):
            bb = c % 2
            if c >= 2:
                P.wait("tensor", s_gev, c - 1)
            for dc in range(NDC):
                P.do("tensor", lambda e, c=c, dc=dc, bb=bb: e.matmul(
                    ps_g[bb][:, 0:E0], xnT[:, dc * S + c * 128:dc * S + (c + 1) * 128],
                    gqT_bf[:, dc * E0:(dc + 1) * E0], start=(dc == 0), stop=(dc == NDC - 1)),
                    inc=(s_gmm, 1) if dc == NDC - 1 else None)
            P.wait("scalar", s_gmm, s_gmm.n)
            if c == 0:
                P.wait("scalar", s_sks, itp_ready)
            P.do("scalar", lambda e, c=c, bb=bb: e.activation(
                tt_sb[:, c * E0:(c + 1) * E0], ps_g[bb][:, 0:E0], AF.Copy,
                scale=sc2[:, 0:1]), inc=(s_gev, 1))
        gev_all = s_gev.n

        P.wait("scalar", s_gev, gev_all)
        tt3 = tt_sb[:].rearrange("p (c e) -> p c e", c=NTC)
        ex3 = ex_sb[:].rearrange("p (c e) -> p c e", c=NTC)
        for it in range(SINK_IT):
            P.do("scalar", lambda e: e.activation(ex_sb[:], tt_sb[:], AF.Exp), inc=(s_sks, 1))
            P.wait("vector", s_sks, s_sks.n)
            P.do("vector", lambda e: e.tensor_reduce(lsn_sb[:], ex3, axis=AX, op=ALU.add),
                 inc=(s_skv, 1))
            P.wait("scalar", s_skv, s_skv.n)
            P.do("scalar", lambda e: e.activation(lsn_sb[:], lsn_sb[:], AF.Ln), inc=(s_sks, 1))
            P.wait("vector", s_sks, s_sks.n)
            P.do("vector", lambda e: e.tensor_tensor(
                tt3, tt3, lsn_sb[:].unsqueeze(2).to_broadcast([128, NTC, E0]),
                op=ALU.subtract), inc=(s_skv, 1))
            P.wait("scalar", s_skv, s_skv.n)
            P.do("scalar", lambda e: e.activation(ex_sb[:], tt_sb[:], AF.Exp), inc=(s_sks, 1))
            P.wait("tensor", s_sks, s_sks.n)
            for c in range(NTC):
                P.do("tensor", lambda e, c=c: e.matmul(
                    psg1[0:1, 0:E0], onesf[:, 0:1], ex_sb[:, c * E0:(c + 1) * E0],
                    start=(c == 0), stop=(c == NTC - 1)),
                    inc=(s_skt, 1) if c == NTC - 1 else None)
            P.wait("scalar", s_skt, s_skt.n)
            P.do("scalar", lambda e: e.activation(sc2[0:1, 1:17], psg1[0:1, 0:E0], AF.Ln),
                 inc=(s_sks, 1))
            P.wait("tensor", s_sks, s_sks.n)
            P.do("tensor", lambda e: e.matmul(psg1[0:128, 0:E0], onesf[0:1, 0:128],
                                              sc2[0:1, 1:17], start=True, stop=True),
                 inc=(s_skt, 1))
            P.wait("vector", s_skt, s_skt.n)
            P.do("vector", lambda e: e.tensor_tensor(
                tt3, tt3,
                psg1[0:128, 0:E0].unsqueeze(1).to_broadcast([128, NTC, E0]),
                op=ALU.subtract), inc=(s_skv, 1))
            P.wait("scalar", s_skv, s_skv.n)
        P.do("scalar", lambda e: e.activation(ex_sb[:], tt_sb[:], AF.Exp), inc=(s_sks, 1))
        aff_ready = s_sks.n

        P.wait("tensor", s_sks, aff_ready)
        for c in range(NTC):
            bb = c % 2
            if c >= 2:
                P.wait("tensor", s_skv, s_skv.n - 1)
            P.do("tensor", lambda e, c=c, bb=bb: e.transpose(
                ps_g[bb][0:16, 0:128], ex_sb[:, c * E0:(c + 1) * E0], idf_sb[:]),
                inc=(s_skt, 1))
            P.wait("vector", s_skt, s_skt.n)
            P.do("vector", lambda e, c=c, bb=bb: e.tensor_copy(
                aff_sb[:, c * 128:(c + 1) * 128], ps_g[bb][0:16, 0:128]), inc=(s_skv, 1))
        affT_done = s_skv.n

        P.wait("vector", s_skv, affT_done)
        src = aff_sb
        for r in range(CAP // 8):
            P.do("vector", lambda e, src=src: e.max(out=m8g[:], in_=src[:]))
            P.drain("vector")
            P.do("vector", lambda e, r=r, src=src: e.tensor_copy(
                gates_sb[:, r * 8:(r + 1) * 8], m8g[:]))
            P.do("vector", lambda e, r=r, src=src: e.max_index(
                out=gidx_sb[:, r * 8:(r + 1) * 8], in_max=m8g[:], in_values=src[:]))
            P.drain("vector")
            P.do("vector", lambda e, src=src: e.match_replace(
                out=affw_sb[:], in_to_replace=m8g[:], in_values=src[:], imm_value=0.0),
                inc=(s_tkv, 1))
            P.drain("vector")
            src = affw_sb
        P.do("vector", lambda e: e.tensor_tensor(G_bf[:], aff_sb[:], affw_sb[:],
                                                 op=ALU.subtract), inc=(s_tkv, 1))
        P.drain("vector")
        P.do("vector", lambda e: e.tensor_copy(affw_sb[:, 0:CAP], gidx_sb[:]), inc=(s_tkv, 1))
        P.wait("tensor", s_tkv, s_tkv.n)
        for cc in range(2):
            P.do("tensor", lambda e, cc=cc: e.transpose(
                psg0[0:128, 0:16], gates_sb[:, cc * 128:(cc + 1) * 128], idf_sb[0:16, 0:16]),
                inc=(s_skt, 1))
            P.wait("vector", s_skt, s_skt.n)
            P.do("vector", lambda e, cc=cc: e.tensor_copy(
                gatesT[:, cc * 16:(cc + 1) * 16], psg0[0:128, 0:16]), inc=(s_skv, 1))
            P.wait("tensor", s_skv, s_skv.n)
            P.do("tensor", lambda e, cc=cc: e.transpose(
                psg1[0:128, 0:16], affw_sb[:, cc * 128:(cc + 1) * 128], idf_sb[0:16, 0:16]),
                inc=(s_skt, 1))
            P.wait("vector", s_skt, s_skt.n)
            P.do("vector", lambda e, cc=cc: e.tensor_copy(
                idxT[:, cc * 16:(cc + 1) * 16], psg1[0:128, 0:16]), inc=(s_skv, 1))
        P.do("vector", lambda e: e.tensor_copy(sc2[:, 20:21], sc2[:, 0:1]), inc=(s_gdone, 1))
    gdone = s_gdone.n
    if dump_cb is not None:
        dump_cb("gating", dict(G_bf=G_bf, gatesT=gatesT, idxT=idxT, s_gdone=s_gdone, cb=cb, sc=sc))
    if stage == "gating":
        return s_gdone

    from contextlib import ExitStack as _ES
    s_wh = [new_sem(nc, stk, pre + f"wh{i}") for i in range(2)]
    s_qkmm = new_sem(nc, stk, pre + "qkmm")
    s_qkev = new_sem(nc, stk, pre + "qkev")
    s_vmm = new_sem(nc, stk, pre + "vmm")
    s_vev = new_sem(nc, stk, pre + "vev")
    s_nsq = new_sem(nc, stk, pre + "nsq")
    s_nmm = new_sem(nc, stk, pre + "nmm")
    s_nrd = new_sem(nc, stk, pre + "nrd")
    s_cbv = new_sem(nc, stk, pre + "cbv")
    s_cbb = new_sem(nc, stk, pre + "cbb")
    s_cbe = new_sem(nc, stk, pre + "cbe")
    s_smm = new_sem(nc, stk, pre + "smm")
    s_pex = new_sem(nc, stk, pre + "pex")
    s_pvk = new_sem(nc, stk, pre + "pvk")
    s_bmm = new_sem(nc, stk, pre + "bmm")
    s_bnd = [new_sem(nc, stk, pre + f"bnd{i}") for i in range(2)]
    s_on = new_sem(nc, stk, pre + "on")
    s_bc = new_sem(nc, stk, pre + "bc")
    s_oev = new_sem(nc, stk, pre + "oev")
    s_att = new_sem(nc, stk, pre + "att")
    s_hfin = new_sem(nc, stk, pre + "hfin")

    with _ES() as at_stk:
        _ac = at_stk.enter_context
        wq_sb = [_ac(nc.sbuf_tensor(f"{pre}wq{i}", [128, NDC * DK], BF16)) for i in range(2)]
        wk_sb = [_ac(nc.sbuf_tensor(f"{pre}wk{i}", [128, NDC * DK], BF16)) for i in range(2)]
        wv_sb = [_ac(nc.sbuf_tensor(f"{pre}wv{i}", [128, NDC * DK], BF16)) for i in range(2)]
        qt_sb = [_ac(nc.sbuf_tensor(f"{pre}qt{i}", [128, S], BF16)) for i in range(2)]
        kt_sb = [_ac(nc.sbuf_tensor(f"{pre}kt{i}", [128, S], BF16)) for i in range(2)]
        v_sb = [_ac(nc.sbuf_tensor(f"{pre}v{i}", [128, NKC * 72], BF16)) for i in range(2)]
        sqsm = [_ac(nc.sbuf_tensor(f"{pre}sqsm{i}", [128, 512], BF16)) for i in range(2)]
        pt_sb = [_ac(nc.sbuf_tensor(f"{pre}pt{i}", [128, 1024], BF16)) for i in range(3)]
        b3_sb = [_ac(nc.sbuf_tensor(f"{pre}b3{i}", [128, 512], BF16)) for i in range(2)]
        rcp_s = _ac(nc.sbuf_tensor(pre + "rcps", [128, 1024], F32))
        otmp = _ac(nc.sbuf_tensor(pre + "otmp", [128, 512], BF16))
        psq = [_ac(nc.psum_tensor(f"{pre}psq{i}", [128, 512], F32)) for i in range(2)]
        ps_sc = [_ac(nc.psum_tensor(f"{pre}pssc{i}", [128, 1024], F32)) for i in range(2)]
        pso = _ac(nc.psum_tensor(pre + "pso", [128, 512], F32))
        psb2 = _ac(nc.psum_tensor(pre + "psb2", [128, 512], F32))
        v0, v1 = v_sb

        P.do("vector", lambda e: e.memset(v0[:], 1.0))
        P.do("vector", lambda e: e.memset(v1[:], 1.0), inc=(s_vev, 1))
        band_ctr = [0]
        band_g = []
        seg_ctr = [0]
        seg_pvk_end = []
        P.wait("gpsimd", s_gdone, gdone)

        def prep_head(h):
            buf = h % 2
            thunks = []

            def t_load():
                if h >= 2:
                    P.wait("gpsimd", s_hfin, h - 1)
                for (dst, key) in ((wq_sb[buf], "wq"), (wk_sb[buf], "wk"), (wv_sb[buf], "wv")):
                    P.do("gpsimd", lambda e, dst=dst, key=key: e.dma_start(
                        dst[:].rearrange("p (dc i) -> p dc i", dc=NDC),
                        w[key].rearrange("(dc p) i -> p dc i", p=128)[:, :, h * DK:(h + 1) * DK]),
                        inc=(s_wh[buf], 16))
            thunks.append(t_load)

            def t_qk_first():
                P.wait("tensor", s_wh[buf], s_wh[buf].n)
                if h >= 2:
                    P.wait("tensor", s_pvk, (h - 1) * 8 * NKC)
                P.wait("tensor", s_nrd, s_nrd.n)
            thunks.append(t_qk_first)

            for mi, (matk, dstk) in enumerate(((0, 0), (1, 1))):
                for t0 in range(0, NQT, 4):
                    def t_qk(mi=mi, t0=t0):
                        mat = (wq_sb[buf], wk_sb[buf])[mi]
                        dst = (qt_sb[buf], kt_sb[buf])[mi]
                        for t in range(t0, t0 + 4):
                            pb = t % 2
                            P.wait("tensor", s_qkev, max(0, s_qkev.n - 1))
                            for dc in range(NDC):
                                P.do("tensor", lambda e, mat=mat, t=t, dc=dc, pb=pb: e.matmul(
                                    psq[pb][0:DK, :], mat[:, dc * DK:(dc + 1) * DK],
                                    xnT[:, dc * S + t * 512:dc * S + (t + 1) * 512],
                                    start=(dc == 0), stop=(dc == NDC - 1)),
                                    inc=(s_qkmm, 1) if dc == NDC - 1 else None)
                            P.wait("vector", s_qkmm, s_qkmm.n)
                            if h >= 2 and mi == 0 and t == 0:
                                P.wait("vector", s_pvk, (h - 1) * 8 * NKC)
                            P.do("vector", lambda e, dst=dst, t=t, pb=pb: e.tensor_copy(
                                dst[0:DK, t * 512:(t + 1) * 512], psq[pb][0:DK, :]),
                                inc=(s_qkev, 1))
                    thunks.append(t_qk)

            for c0 in range(0, NKC, 16):
                def t_v(c0=c0):
                    if c0 == 0:
                        P.wait("tensor", s_qkev, s_qkev.n)
                    for c in range(c0, c0 + 16):
                        vb = c % 2
                        P.wait("tensor", s_vev, max(1, s_vev.n - 1))
                        for dc in range(NDC):
                            P.do("tensor", lambda e, c=c, dc=dc, vb=vb: e.matmul(
                                psq[vb][:, 0:DK], xnT[:, dc * S + c * 128:dc * S + (c + 1) * 128],
                                wv_sb[buf][:, dc * DK:(dc + 1) * DK],
                                start=(dc == 0), stop=(dc == NDC - 1)),
                                inc=(s_vmm, 1) if dc == NDC - 1 else None)
                        P.wait("vector", s_vmm, s_vmm.n)
                        P.do("vector", lambda e, c=c, vb=vb: e.tensor_copy(
                            v_sb[buf][:, c * 72:c * 72 + DK], psq[vb][:, 0:DK]),
                            inc=(s_vev, 1))
                thunks.append(t_v)

            def t_norms():
                P.wait("tensor", s_vev, s_vev.n)
                for widx in range(2):
                    src = (qt_sb[buf], kt_sb[buf])[widx]
                    P.wait("gpsimd", s_qkev, s_qkev.n)
                    for t in range(NQT):
                        pb = t % 2
                        P.wait("gpsimd", s_nmm, max(0, s_nmm.n - 1))
                        P.do("gpsimd", lambda e, src=src, t=t, pb=pb: e.tensor_mul(
                            sqsm[pb][0:DK, :], src[0:DK, t * 512:(t + 1) * 512],
                            src[0:DK, t * 512:(t + 1) * 512]), inc=(s_nsq, 1))
                        P.wait("tensor", s_nsq, s_nsq.n)
                        if s_nrd.n >= 1:
                            P.wait("tensor", s_nrd, max(0, s_nrd.n - 1))
                        P.do("tensor", lambda e, t=t, pb=pb: e.matmul(
                            psq[pb][0:1, :], ones_bf[0:DK, 0:1], sqsm[pb][0:DK, :],
                            start=True, stop=True), inc=(s_nmm, 1))
                        P.wait("vector", s_nmm, s_nmm.n)
                        P.do("vector", lambda e, t=t, pb=pb: e.tensor_reduce(
                            sc[0:1, t:t + 1], psq[pb][0:1, :], axis=AX, op=ALU.max),
                            inc=(s_nrd, 1))
                    P.wait("vector", s_nrd, s_nrd.n)
                    P.drain("vector")
                    P.do("vector", lambda e, widx=widx: e.tensor_reduce(
                        sc[0:1, 8 + widx:9 + widx], sc[0:1, 0:NQT], axis=AX, op=ALU.max),
                        inc=(s_nrd, 1))
                    P.drain("vector")
            thunks.append(t_norms)

            def t_cb():
                P.do("vector", lambda e: e.tensor_scalar(sc[0:1, 10:11], sc[0:1, 8:9], 0.5,
                                                         None, op0=ALU.mult))
                P.drain("vector")
                P.do("vector", lambda e: e.scalar_tensor_tensor(sc[0:1, 11:12], sc[0:1, 9:10],
                                                                0.5, sc[0:1, 10:11],
                                                                op0=ALU.mult, op1=ALU.add))
                P.drain("vector")
                P.do("vector", lambda e: e.tensor_tensor(sc[0:1, 12:13], sc[0:1, 11:12],
                                                         sc[0:1, 31:32], op=ALU.add))
                P.drain("vector")
                P.do("vector", lambda e: e.tensor_scalar(sc[0:1, 13:14], sc[0:1, 12:13], -1.0,
                                                         None, op0=ALU.mult))
                P.drain("vector")
                P.do("vector", lambda e: e.tensor_tensor(sc[0:1, 14:15], trow_l[0:1, h:h + 1],
                                                         sc[0:1, 13:14], op=ALU.add))
                P.do("vector", lambda e: e.tensor_tensor(sc[0:1, 15:16], trow_r[0:1, h:h + 1],
                                                         sc[0:1, 13:14], op=ALU.add),
                     inc=(s_cbv, 1))
                P.drain("vector")
                cbv_i = s_cbv.n
                P.wait("tensor", s_cbv, cbv_i)
                for j, col in ((0, 14), (1, 15), (2, 13)):
                    P.wait("tensor", s_cbe, s_cbe.n)
                    P.wait("tensor", s_nrd, s_nrd.n)
                    P.do("tensor", lambda e, col=col: e.matmul(
                        psq[0][0:128, 0:1], onesf[0:1, 0:128], sc[0:1, col:col + 1],
                        start=True, stop=True), inc=(s_cbb, 1))
                    P.wait("vector", s_cbb, s_cbb.n)
                    P.do("vector", lambda e, j=j: e.tensor_copy(
                        cb[:, 3 * h + j:3 * h + j + 1], psq[0][0:128, 0:1]), inc=(s_cbe, 1))
            thunks.append(t_cb)
            return thunks

        for t in prep_head(0):
            t()
        for h in range(NH):
            buf = h % 2
            nxt = prep_head(h + 1) if h + 1 < NH else []
            P.wait("tensor", s_cbe, s_cbe.n)
            P.wait("scalar", s_cbe, s_cbe.n)
            for qt_ in range(NQT):
                if h + qt_ > 0:
                    P.wait("tensor", s_oev, s_oev.n)
                    P.wait("tensor", s_on, s_on.n)
                kinds = [band_kind(qt_, kc) for kc in range(NKC)]
                segs = []
                kc = 0
                while kc < NKC:
                    if kc + 1 < NKC and kinds[kc + 1] == kinds[kc]:
                        segs.append((kc, 2, kinds[kc]))
                        kc += 2
                    else:
                        segs.append((kc, 1, kinds[kc]))
                        kc += 1

                def emit_pv(ent):
                    (slot_, kc0_, w2_, pex_t, vbuf) = ent
                    P.wait("tensor", s_pex, pex_t)
                    for ki in range(w2_):
                        kc = kc0_ + ki
                        P.do("tensor", lambda e, kc=kc, slot_=slot_, ki=ki, vbuf=vbuf: e.matmul(
                            pso[0:65, :], v_sb[vbuf][:, kc * 72:kc * 72 + 65],
                            pt_sb[slot_][:, ki * 512:(ki + 1) * 512],
                            start=(kc == 0), stop=(kc == NKC - 1)), inc=(s_pvk, 1))
                    seg_pvk_end.append(s_pvk.n)

                pending_pv = []
                for (kc0, w_, kind) in segs:
                    sg = seg_ctr[0]
                    seg_ctr[0] += 1
                    pp = sg % 2
                    slot = sg % 3
                    if sg >= 2:
                        P.wait("tensor", s_pex, sg - 1)
                    for ki in range(w_):
                        kc = kc0 + ki
                        P.do("tensor", lambda e, buf=buf, kc=kc, qt_=qt_, pp=pp, ki=ki,
                             kind=kind: e.matmul(
                                 ps_sc[pp][:, ki * 512:(ki + 1) * 512],
                                 kt_sb[buf][0:DK, kc * 128:(kc + 1) * 128],
                                 qt_sb[buf][0:DK, qt_ * 512:(qt_ + 1) * 512],
                                 start=True, stop=(kind != 2)),
                             inc=(s_smm, 1) if kind != 2 else None)
                        if kind == 2:
                            bi = band_ctr[0]
                            band_ctr[0] += 1
                            band_g.append(sg)
                            bslot = bi % 2
                            if bi >= 2:
                                P.wait("sync", s_pex, band_g[bi - 2] + 1)
                            e0 = 3968 - 128 * kc + 512 * qt_
                            src = wbias_dram.copy()
                            src.offset = h * 8191 + e0
                            src.ap = bass_rust.VecI64Pair([[1, 128], [1, 512]])
                            if bi == 0:
                                P.wait("sync", s_wbw, wbw_all)
                            P.do("sync", lambda e, bslot=bslot, src=src: e.dma_start(
                                b3_sb[bslot][:], src), inc=(s_bnd[bslot], 16))
                            P.wait("tensor", s_bnd[bslot], s_bnd[bslot].n)
                            P.do("tensor", lambda e, pp=pp, ki=ki, bslot=bslot: e.matmul(
                                ps_sc[pp][:, ki * 512:(ki + 1) * 512], J_sb[:],
                                b3_sb[bslot][:], start=False, stop=True),
                                inc=(s_smm, 1))
                    cbcol = 3 * h + {0: 0, 1: 1, 2: 2}[kind]
                    P.wait("scalar", s_smm, s_smm.n)
                    if sg >= 3:
                        P.wait("scalar", s_pvk, seg_pvk_end[sg - 3])
                    P.do("scalar", lambda e, pp=pp, slot=slot, w_=w_, cbcol=cbcol: e.activation(
                        pt_sb[slot][:, 0:w_ * 512], ps_sc[pp][:, 0:w_ * 512], AF.Exp,
                        bias=cb[:, cbcol:cbcol + 1]), inc=(s_pex, 1))
                    pex_t = s_pex.n
                    pending_pv.append((slot, kc0, w_, pex_t, buf))
                    if len(pending_pv) > 2:
                        emit_pv(pending_pv.pop(0))
                while pending_pv:
                    emit_pv(pending_pv.pop(0))
                pv_i = s_pvk.n
                P.wait("vector", s_pvk, pv_i)
                P.do("vector", lambda e: e.tensor_scalar(rcp_s[64:65, 0:512], pso[64:65, :],
                                                         1e-37, None, op0=ALU.max))
                P.drain("vector")
                P.do("vector", lambda e: e.reciprocal(rcp_s[64:65, 512:1024],
                                                      rcp_s[64:65, 0:512]))
                P.drain("vector")
                P.do("vector", lambda e: e.tensor_copy(rcp_s[64:65, 0:512],
                                                       rcp_s[64:65, 512:1024]), inc=(s_on, 1))
                P.wait("tensor", s_on, s_on.n)
                P.wait("tensor", s_att, s_att.n)
                P.do("tensor", lambda e: e.matmul(
                    psb2[0:64, :], onesf[64:65, 0:64], rcp_s[64:65, 0:512],
                    start=True, stop=True), inc=(s_bc, 1))
                P.wait("vector", s_pvk, pv_i)
                P.wait("vector", s_att, s_att.n)
                P.do("vector", lambda e: e.tensor_copy(otmp[0:DK, :], pso[0:DK, :]),
                     inc=(s_oev, 1))
                P.wait("vector", s_bc, s_bc.n)
                P.wait("vector", s_oev, s_oev.n)
                rows = (h % 2) * 64
                cols = (h // 2) * S + qt_ * 512
                P.do("vector", lambda e, rows=rows, cols=cols: e.tensor_tensor(
                    attnT[rows:rows + 64, cols:cols + 512], otmp[0:DK, :], psb2[0:64, :],
                    op=ALU.mult), inc=(s_att, 1))
                n_per = (len(nxt) + NQT - 1) // NQT if nxt else 0
                for _ in range(n_per):
                    if nxt:
                        nxt.pop(0)()
            while nxt:
                nxt.pop(0)()
            P.do("tensor", lambda e: e.nop(), inc=(s_hfin, 1))
    att_all = s_att.n
    hfin_all = s_hfin.n
    if dump_cb is not None:
        dump_cb("attn", dict(attnT=attnT, s_att=s_att, cb=cb, sc=sc))
    if stage == "attn":
        return s_att
    xnT_stk.close()

    s_gat = new_sem(nc, stk, pre + "gat")
    s_etr = new_sem(nc, stk, pre + "etr")
    s_etv = new_sem(nc, stk, pre + "etv")
    s_ewq = [new_sem(nc, stk, pre + f"ewq{i}") for i in range(2)]
    s_eqf = new_sem(nc, stk, pre + "eqf")
    s_hmm = new_sem(nc, stk, pre + "hmm")
    s_gel = new_sem(nc, stk, pre + "gel")
    s_dmm = new_sem(nc, stk, pre + "dmm")
    s_stg = new_sem(nc, stk, pre + "stg")
    s_sca = new_sem(nc, stk, pre + "sca")

    ex_stk = ExitStack()
    _ec = ex_stk.enter_context
    ein_sb = _ec(nc.sbuf_tensor(pre + "ein", [128, 2 * D], BF16))
    einT = _ec(nc.sbuf_tensor(pre + "einT", [128, NDC * CAP], BF16))
    w1q_sb = [_ec(nc.sbuf_tensor(f"{pre}w1q{i}", [128, NDC * 1024], BF16)) for i in range(2)]
    w2q_sb = [_ec(nc.sbuf_tensor(f"{pre}w2q{i}", [128, 8 * D], BF16)) for i in range(2)]
    hT = _ec(nc.sbuf_tensor(pre + "hT", [128, 8 * CAP], BF16))
    b1T_sb = _ec(nc.sbuf_tensor(pre + "b1T", [128, 32 * E0], F32))
    scs = [_ec(nc.sbuf_tensor(f"{pre}scs{i}", [128, D], F32)) for i in range(2)]
    pst = [_ec(nc.psum_tensor(f"{pre}pstr{i}", [128, 128], BF16)) for i in range(2)]

    def load_equarter(qi, e_, q_):
        buf = qi % 2
        if qi >= 2:
            P.wait("gpsimd", s_eqf, qi - 1)
        P.do("gpsimd", lambda e, e_=e_, q_=q_, buf=buf: e.dma_start(
            w1q_sb[buf][:].rearrange("p (dc f) -> p dc f", dc=NDC),
            w["w1"][e_].rearrange("(dc p) f -> p dc f", p=128)[:, :, q_ * 1024:(q_ + 1) * 1024]),
            inc=(s_ewq[buf], 16))
        P.do("gpsimd", lambda e, e_=e_, q_=q_, buf=buf: e.dma_start(
            w2q_sb[buf][:].rearrange("p (fc d) -> p fc d", fc=8),
            w["w2"][e_][q_ * 1024:(q_ + 1) * 1024, :].rearrange("(fc p) d -> p fc d", p=128)),
            inc=(s_ewq[buf], 16))

    def gather_expert(e_):
        if e_ >= 1:
            P.wait("gpsimd", s_etr, e_ * 16)
        for cc in range(2):
            P.do("gpsimd", lambda e, e_=e_, cc=cc: e.indirect_dma_start(
                ein_sb[:, cc * D:(cc + 1) * D], None, xn_dram,
                IOA(idxT[:, cc * E0 + e_:cc * E0 + e_ + 1], 0)), inc=(s_gat, 16))

    def transpose_expert(e_, pst):
        gat_i = s_gat.n
        P.wait("tensor", s_gat, gat_i)
        if e_ >= 1:
            P.wait("tensor", s_hmm, e_ * 32)
        for i2 in range(16):
            cc, dc = i2 // NDC, i2 % NDC
            tb = i2 % 2
            if i2 >= 2:
                P.wait("tensor", s_etv, s_etv.n - 1)
            P.do("tensor", lambda e, cc=cc, dc=dc, tb=tb: e.transpose(
                pst[tb][:], ein_sb[:, cc * D + dc * 128: cc * D + (dc + 1) * 128],
                idb_sb[:]), inc=(s_etr, 1))
            P.wait("vector", s_etr, s_etr.n)
            P.do("vector", lambda e, cc=cc, dc=dc, tb=tb: e.tensor_copy(
                einT[:, dc * CAP + cc * 128: dc * CAP + (cc + 1) * 128], pst[tb][:]),
                inc=(s_etv, 1))

    equarters = [(e_, q_) for e_ in range(E0) for q_ in range(4)]
    P.wait("gpsimd", s_att, att_all)
    P.wait("gpsimd", s_gdone, gdone)
    P.do("gpsimd", lambda e: e.dma_start(b1T_sb[:], w["b1T"]), inc=(s_gat, 16))
    gather_expert(0)
    load_equarter(0, *equarters[0])
    load_equarter(1, *equarters[1])

    s_wo = new_sem(nc, stk, pre + "wo")
    s_omm = new_sem(nc, stk, pre + "omm")
    s_xld = [new_sem(nc, stk, pre + f"xld{i}") for i in range(2)]
    s_sv = new_sem(nc, stk, pre + "sv")
    s_x1w = [new_sem(nc, stk, pre + f"x1w{i}") for i in range(2)]

    with _ES() as op_stk:
        _oc = op_stk.enter_context
        wo_sb = _oc(nc.sbuf_tensor(pre + "wo", [128, NDC * D], BF16))
        b2_bf = _oc(nc.sbuf_tensor(pre + "b2", [16, D], BF16))
        xld = [_oc(nc.sbuf_tensor(f"{pre}xld{i}", [128, D], F32)) for i in range(2)]
        seed = [_oc(nc.sbuf_tensor(f"{pre}seed{i}", [128, D], F32)) for i in range(2)]
        po = [_oc(nc.psum_tensor(f"{pre}po{i}", [128, D], F32)) for i in range(2)]
        P.wait("gpsimd", s_att, att_all)
        P.wait("sync", s_att, att_all)
        for dc in range(NDC):
            P.do("gpsimd", lambda e, dc=dc: e.dma_start(
                wo_sb[:, dc * D:(dc + 1) * D], w["wo"][dc * 128:(dc + 1) * 128, :]),
                inc=(s_wo, 16))
        P.do("gpsimd", lambda e: e.dma_start(b2_bf[:], w["b2"]), inc=(s_wo, 16))
        wo_all = s_wo.n
        P.wait("tensor", s_wo, wo_all)
        P.wait("tensor", s_gdone, gdone)
        for tc in range(NTC):
            bb = tc % 2
            if tc >= 2:
                P.wait("sync", s_sv, 2 * (tc - 1))
            P.do("sync", lambda e, bb=bb, tc=tc: e.dma_start(
                xld[bb][:], x_dram[tc * 128:(tc + 1) * 128, :]), inc=(s_xld[bb], 16))
            xld_i = s_xld[bb].n
            if tc >= 2:
                P.wait("tensor", s_sv, 2 * (tc - 1))
            for half in range(2):
                for ic in range(NDC):
                    P.do("tensor", lambda e, bb=bb, tc=tc, half=half, ic=ic: e.matmul(
                        po[bb][:, half * 512:(half + 1) * 512],
                        attnT[:, ic * S + tc * 128: ic * S + (tc + 1) * 128],
                        wo_sb[:, ic * D + half * 512: ic * D + (half + 1) * 512],
                        start=(ic == 0), stop=False))
                P.do("tensor", lambda e, bb=bb, tc=tc, half=half: e.matmul(
                    po[bb][:, half * 512:(half + 1) * 512], G_bf[0:16, tc * 128:(tc + 1) * 128],
                    b2_bf[0:16, half * 512:(half + 1) * 512], start=False, stop=True),
                    inc=(s_omm, 1))
            omm_i = s_omm.n
            P.wait("vector", s_omm, omm_i)
            P.wait("vector", s_xld[bb], xld_i)
            if tc >= 2:
                P.wait("vector", s_x1w[bb], ((tc - 2) // 2 + 1) * 16)
            for half in range(2):
                P.do("vector", lambda e, bb=bb, half=half: e.tensor_tensor(
                    seed[bb][:, half * 512:(half + 1) * 512],
                    po[bb][:, half * 512:(half + 1) * 512],
                    xld[bb][:, half * 512:(half + 1) * 512], op=ALU.add), inc=(s_sv, 1))
            P.wait("gpsimd", s_sv, s_sv.n)
            P.do("gpsimd", lambda e, bb=bb, tc=tc: e.dma_start(
                x1_dram[tc * 128:(tc + 1) * 128, :], seed[bb][:]), inc=(s_x1w[bb], 16))
        omm_all = s_omm.n
    if stage == "oproj":
        wait_all(P, "gpsimd", s_x1w)
        return s_x1w

    if True:
        psh = [ex_stk.enter_context(nc.psum_tensor(f"{pre}psh{i}", [128, 512], F32))
               for i in range(2)]
        psd = [ex_stk.enter_context(nc.psum_tensor(f"{pre}psd{i}", [128, D], F32))
               for i in range(2)]
        transpose_expert(0, pst)
        P.wait("tensor", s_sv, s_sv.n)
        for qi, (e_, q_) in enumerate(equarters):
            buf = qi % 2
            P.wait("tensor", s_ewq[buf], (qi // 2 + 1) * 32)
            if q_ == 0:
                P.wait("tensor", s_etv, s_etv.n)
            for fc in range(8):
                hb = fc % 2
                gidx = qi * 8 + fc
                if gidx >= 2:
                    P.wait("tensor", s_gel, gidx - 1)
                for dc in range(NDC):
                    P.do("tensor", lambda e, buf=buf, fc=fc, dc=dc, hb=hb: e.matmul(
                        psh[hb][:, 0:CAP],
                        w1q_sb[buf][:, dc * 1024 + fc * 128: dc * 1024 + (fc + 1) * 128],
                        einT[:, dc * CAP:(dc + 1) * CAP],
                        start=(dc == 0), stop=(dc == NDC - 1)),
                        inc=(s_hmm, 1) if dc == NDC - 1 else None)
                P.wait("scalar", s_hmm, s_hmm.n)
                if qi >= 1 and fc == 0:
                    P.wait("scalar", s_eqf, qi)
                P.do("scalar", lambda e, e_=e_, q_=q_, fc=fc, hb=hb: e.activation(
                    hT[:, fc * CAP:(fc + 1) * CAP], psh[hb][:, 0:CAP], AF.Gelu_apprx_tanh,
                    bias=b1T_sb[:, (q_ * 8 + fc) * E0 + e_:(q_ * 8 + fc) * E0 + e_ + 1]),
                    inc=(s_gel, 1))
            gel_i = s_gel.n
            P.wait("tensor", s_gel, gel_i)
            if q_ == 0 and e_ >= 1:
                P.wait("tensor", s_stg, e_ * 2)
            for cc in range(2):
                for fc in range(8):
                    for half in range(2):
                        last = (cc == 1 and fc == 7 and half == 1)
                        P.do("tensor", lambda e, buf=buf, cc=cc, fc=fc, half=half, q_=q_: e.matmul(
                            psd[cc][:, half * 512:(half + 1) * 512],
                            hT[:, fc * CAP + cc * 128: fc * CAP + (cc + 1) * 128],
                            w2q_sb[buf][:, fc * D + half * 512: fc * D + (half + 1) * 512],
                            start=(q_ == 0 and fc == 0), stop=(q_ == 3 and fc == 7)),
                            inc=(s_eqf, 1) if last else None)
            if qi + 2 < len(equarters):
                load_equarter(qi + 2, *equarters[qi + 2])
            if q_ == 3:
                P.wait("scalar", s_eqf, qi + 1)
                for cc in range(2):
                    if e_ >= 1 and cc == 0:
                        P.wait("scalar", s_sca, e_ * 32)
                    P.do("scalar", lambda e, e_=e_, cc=cc: e.activation(
                        scs[cc][:], psd[cc][:], AF.Copy,
                        scale=gatesT[:, cc * E0 + e_:cc * E0 + e_ + 1]), inc=(s_stg, 1))
                stg_i = s_stg.n
                P.wait("gpsimd", s_stg, stg_i)
                if e_ == 0:
                    wait_all(P, "gpsimd", s_x1w)
                for cc in range(2):
                    P.do("gpsimd", lambda e, e_=e_, cc=cc: e.indirect_dma_start(
                        x1_dram, IOA(idxT[:, cc * E0 + e_:cc * E0 + e_ + 1], 0),
                        scs[cc][:], None, compute_op=ALU.add), inc=(s_sca, 16))
                if e_ + 1 < E0:
                    gather_expert(e_ + 1)
                    transpose_expert(e_ + 1, pst)
    P.wait("gpsimd", s_sca, s_sca.n)
    ex_stk.close()
    l0t.close()
    return s_sca


from contextlib import ExitStack
from concourse.bass_utils import run_bass_kernel_spmd

_HC = None


def _hc():
    global _HC
    if _HC is None:
        _HC = host_consts()
    return _HC


def _build_l0_program():
    nc = bass.Bass("TRN2", target_bir_lowering=False)
    x_d = nc.dram_tensor("x", [S, D], F32, kind="ExternalInput").ap()
    x1_d = nc.dram_tensor("x1", [S, D], F32, kind="ExternalOutput").ap()
    w = {}
    for name, shape, dt in [
        ("wq", [D, D], F32), ("wk", [D, D], F32), ("wv", [D, D], F32), ("wo", [D, D], F32),
        ("table", [32, NH], F32), ("tabflat", [512], F32), ("oneh", [32, 8191], BF16),
        ("J", [128, 128], BF16), ("identf", [128, 128], F32), ("identb", [128, 128], BF16),
        ("gqT", [D, E0], F32), ("ltemp", [1], F32),
        ("w1", [E0, D, 4096], F32), ("b1T", [128, 32 * E0], F32),
        ("w2", [E0, 4096, D], F32), ("b2", [E0, D], F32),
    ]:
        w[name] = nc.dram_tensor(name, shape, dt, kind="ExternalInput").ap()
    with ExitStack() as stk:
        with nc.Block() as block:
            P = Prog(nc)
            build_l0(nc, P, stk, x_d, x1_d, w)
            P.finalize(block)
    return nc


def _build_fused_program():
    nc = bass.Bass("TRN2", target_bir_lowering=False)
    x_d = nc.dram_tensor("x", [S, D], F32, kind="ExternalInput").ap()
    out_d = nc.dram_tensor("out", [S, D], F32, kind="ExternalOutput").ap()
    x1_d = nc.dram_tensor("x1mid", [S, D], F32, kind="Internal").ap()
    w0 = {}
    for name, shape, dt in [
        ("wq", [D, D], F32), ("wk", [D, D], F32), ("wv", [D, D], F32), ("wo", [D, D], F32),
        ("table", [32, NH], F32), ("tabflat", [512], F32), ("oneh", [32, 8191], BF16),
        ("J", [128, 128], BF16), ("identf", [128, 128], F32), ("identb", [128, 128], BF16),
        ("gqT", [D, E0], F32), ("ltemp", [1], F32),
        ("w1", [E0, D, 4096], F32), ("b1T", [128, 32 * E0], F32),
        ("w2", [E0, 4096, D], F32), ("b2", [E0, D], F32),
    ]:
        w0[name] = nc.dram_tensor(name, shape, dt, kind="ExternalInput").ap()
    w1 = {
        "A2": nc.dram_tensor("A2", [128, D // 128], F32, kind="ExternalInput").ap(),
        "BT": nc.dram_tensor("BT", [D, D], F32, kind="ExternalInput").ap(),
        "WT": nc.dram_tensor("WT", [D, D], F32, kind="ExternalInput").ap(),
        "b": nc.dram_tensor("b", [D], F32, kind="ExternalInput").ap(),
        "rw": nc.dram_tensor("rw", [D, 8], F32, kind="ExternalInput").ap(),
        "up": nc.dram_tensor("up", [8, D, 4096], F32, kind="ExternalInput").ap(),
        "dn": nc.dram_tensor("dn", [8, 4096, D], F32, kind="ExternalInput").ap(),
    }
    with ExitStack() as stk:
        with nc.Block() as block:
            P = Prog(nc)
            s_sca = build_l0(nc, P, stk, x_d, x1_d, w0, pre="A_")
            for eng in Prog.ENGS:
                P.wait(eng, s_sca, s_sca.n)
            build_l1(nc, P, stk, x1_d, out_d, w1, pre="B_")
            P.finalize(block)
    return nc


def _build_l1_program():
    nc = bass.Bass("TRN2", target_bir_lowering=False)
    x1_d = nc.dram_tensor("x1", [S, D], F32, kind="ExternalInput").ap()
    out_d = nc.dram_tensor("out", [S, D], F32, kind="ExternalOutput").ap()
    w = {
        "A2": nc.dram_tensor("A2", [128, D // 128], F32, kind="ExternalInput").ap(),
        "BT": nc.dram_tensor("BT", [D, D], F32, kind="ExternalInput").ap(),
        "WT": nc.dram_tensor("WT", [D, D], F32, kind="ExternalInput").ap(),
        "b": nc.dram_tensor("b", [D], F32, kind="ExternalInput").ap(),
        "rw": nc.dram_tensor("rw", [D, 8], F32, kind="ExternalInput").ap(),
        "up": nc.dram_tensor("up", [8, D, 4096], F32, kind="ExternalInput").ap(),
        "dn": nc.dram_tensor("dn", [8, 4096, D], F32, kind="ExternalInput").ap(),
    }
    with ExitStack() as stk:
        with nc.Block() as block:
            P = Prog(nc)
            build_l1(nc, P, stk, x1_d, out_d, w)
            P.finalize(block)
    return nc


def kernel(
    L0_attn_hidden_states, L0_attn_w_q, L0_attn_w_k, L0_attn_w_v, L0_attn_w_o,
    L0_attn_rel_bias_table, L0_ffn_gate_query, L0_ffn_log_temp, L0_ffn_w1, L0_ffn_b1,
    L0_ffn_w2, L0_ffn_b2, L1_attn_A, L1_attn_B, L1_attn_W, L1_attn_b,
    L1_ffn_router_w, L1_ffn_up_w, L1_ffn_down_w,
):
    NCORES = 8
    hc = _hc()
    x_full = np.ascontiguousarray(np.asarray(L0_attn_hidden_states, np.float32))
    table = np.ascontiguousarray(np.asarray(L0_attn_rel_bias_table, np.float32))
    b1 = np.asarray(L0_ffn_b1, np.float32)
    b1T = np.ascontiguousarray(b1.T.reshape(32, 128, E0).transpose(1, 0, 2).reshape(128, 32 * E0))
    w_common0 = dict(
        wq=np.ascontiguousarray(np.asarray(L0_attn_w_q, np.float32)),
        wk=np.ascontiguousarray(np.asarray(L0_attn_w_k, np.float32)),
        wv=np.ascontiguousarray(np.asarray(L0_attn_w_v, np.float32)),
        wo=np.ascontiguousarray(np.asarray(L0_attn_w_o, np.float32)),
        table=table, tabflat=np.ascontiguousarray(table.flatten()),
        oneh=hc["oneh"], J=hc["J"], identf=hc["identf"], identb=hc["identb"],
        gqT=np.ascontiguousarray(np.asarray(L0_ffn_gate_query, np.float32).T),
        ltemp=np.asarray(L0_ffn_log_temp, np.float32).reshape(1),
        w1=np.ascontiguousarray(np.asarray(L0_ffn_w1, np.float32)), b1T=b1T,
        w2=np.ascontiguousarray(np.asarray(L0_ffn_w2, np.float32)),
        b2=np.ascontiguousarray(np.asarray(L0_ffn_b2, np.float32)),
    )
    w_common1 = dict(
        A2=np.ascontiguousarray(np.asarray(L1_attn_A, np.float32).reshape(-1, 128).T),
        BT=np.ascontiguousarray(np.asarray(L1_attn_B, np.float32).T),
        WT=np.ascontiguousarray(np.asarray(L1_attn_W, np.float32).T),
        b=np.ascontiguousarray(np.asarray(L1_attn_b, np.float32)),
        rw=np.ascontiguousarray(np.asarray(L1_ffn_router_w, np.float32)),
        up=np.ascontiguousarray(np.asarray(L1_ffn_up_w, np.float32)),
        dn=np.ascontiguousarray(np.asarray(L1_ffn_down_w, np.float32)),
    )
    nc_f = _build_fused_program()
    in_maps = [dict(x=np.ascontiguousarray(x_full[c]), **w_common0, **w_common1)
               for c in range(NCORES)]
    res = run_bass_kernel_spmd(nc_f, in_maps, core_ids=list(range(NCORES)))
    out = np.stack([np.asarray(r["out"]) for r in res.results], axis=0)
    return out.astype(np.float32)


# version 11
# speedup vs baseline: 1.0834x; 1.0834x over previous
import numpy as np
import ml_dtypes
import bass_rust
import concourse.bass as bass
import concourse.mybir as mybir
from concourse.bass_core_rust import IndirectOffsetOnAxis as IOA

F32 = mybir.dt.float32
BF16 = mybir.dt.bfloat16
I32 = mybir.dt.int32
U32 = mybir.dt.uint32
AF = mybir.ActivationFunctionType
ALU = mybir.AluOpType
AX = mybir.AxisListType.X

S = 4096
D = 1024
NTC = S // 128
NDC = D // 128


class Sem:
    def __init__(self, h):
        self.h = h
        self.n = 0


class Prog:
    ENGS = ("tensor", "vector", "scalar", "gpsimd", "sync")

    def __init__(self, nc):
        self.nc = nc
        self.q = {e: [] for e in self.ENGS}

    def do(self, eng, fn, inc=None, inc2=None):
        if inc is not None:
            inc[0].n += inc[1]
        if inc2 is not None:
            inc2[0].n += inc2[1]

        def emit(e, fn=fn, inc=inc, inc2=inc2):
            inst = fn(e)
            if inc is not None:
                inst.then_inc(inc[0].h, inc[1])
            if inc2 is not None:
                inst.then_inc(inc2[0].h, inc2[1])
        self.q[eng].append(emit)
        return inc[0].n if inc is not None else None

    def drain(self, eng):
        self.q[eng].append(lambda e: e.drain())

    def wait(self, eng, sem, val=None):
        v = sem.n if val is None else val
        assert v >= 0
        self.q[eng].append(lambda e, s=sem.h, v=v: e.wait_ge(s, v))

    def finalize(self, block):
        for name in self.ENGS:
            fns = self.q[name]
            getattr(block, name)(lambda eng, fns=fns: [f(eng) for f in fns])


def new_sem(nc, stk, name):
    return Sem(stk.enter_context(nc.semaphore(name)))


def emit_ln(nc, P, pre, x_dram, xn_dram, tstk, sstk):
    xt = [tstk.enter_context(nc.sbuf_tensor(f"{pre}xt{i}", [128, D], F32)) for i in range(2)]
    xnb = [tstk.enter_context(nc.sbuf_tensor(f"{pre}xnb{i}", [128, D], BF16)) for i in range(2)]
    junk = tstk.enter_context(nc.sbuf_tensor(f"{pre}junk", [128, D], F32))
    st = tstk.enter_context(nc.sbuf_tensor(f"{pre}st", [128, 12], F32))
    s_ld = [new_sem(nc, sstk, pre + f"ld{i}") for i in range(2)]
    s_st = [new_sem(nc, sstk, pre + f"st{i}") for i in range(2)]
    s_sq = new_sem(nc, sstk, pre + "sq")
    s_rs = new_sem(nc, sstk, pre + "rs")
    s_v = new_sem(nc, sstk, pre + "v")
    s_fr = new_sem(nc, sstk, pre + "fr")

    P.do("vector", lambda e: e.memset(st[:, 8:9], 1e-5), inc=(s_v, 1))
    for i in range(NTC):
        b = i % 2
        if i >= 2:
            P.wait("sync", s_fr, i - 1)
        P.do("sync", lambda e, b=b, i=i: e.dma_start(xt[b][:], x_dram[i * 128:(i + 1) * 128, :]),
             inc=(s_ld[b], 16))
        ld_i = s_ld[b].n
        P.wait("scalar", s_ld[b], ld_i)
        P.do("scalar", lambda e, b=b: e.activation(junk[:], xt[b][:], AF.Square,
                                                   accum_out=st[:, 2:3]), inc=(s_sq, 1))
        P.drain("scalar")
        sq_i = s_sq.n
        P.wait("vector", s_ld[b], ld_i)
        P.do("vector", lambda e, b=b: e.tensor_reduce(st[:, 0:1], xt[b][:], axis=AX, op=ALU.add))
        P.drain("vector")
        P.do("vector", lambda e: e.tensor_scalar(st[:, 1:2], st[:, 0:1], -1.0 / D, None,
                                                 op0=ALU.mult))
        P.drain("vector")
        P.wait("vector", s_sq, sq_i)
        P.do("vector", lambda e: e.tensor_scalar(st[:, 3:4], st[:, 2:3], 1.0 / D, None,
                                                 op0=ALU.mult))
        P.do("vector", lambda e: e.tensor_tensor(st[:, 4:5], st[:, 1:2], st[:, 1:2], op=ALU.mult))
        P.drain("vector")
        P.do("vector", lambda e: e.tensor_tensor(st[:, 5:6], st[:, 3:4], st[:, 4:5],
                                                 op=ALU.subtract), inc=(s_v, 1))
        var_i = s_v.n
        P.wait("scalar", s_v, var_i)
        P.do("scalar", lambda e: e.activation(st[:, 6:7], st[:, 5:6], AF.Ln, bias=st[:, 8:9]))
        P.drain("scalar")
        P.do("scalar", lambda e: e.activation(st[:, 7:8], st[:, 6:7], AF.Exp, scale=-0.5),
             inc=(s_rs, 1))
        rs_i = s_rs.n
        P.wait("vector", s_rs, rs_i)
        if i >= 2:
            P.wait("vector", s_st[b], ((i - 2) // 2 + 1) * 16)
        P.do("vector", lambda e, b=b: e.tensor_scalar(xnb[b][:], xt[b][:], st[:, 1:2], st[:, 7:8],
                                                      op0=ALU.add, op1=ALU.mult), inc=(s_fr, 1))
        fr_i = s_fr.n
        P.wait("sync", s_fr, fr_i)
        P.do("sync", lambda e, b=b, i=i: e.dma_start(xn_dram[i * 128:(i + 1) * 128, :], xnb[b][:]),
             inc=(s_st[b], 16))
    return s_st


def wait_all(P, eng, sems):
    for sm in sems:
        P.wait(eng, sm, sm.n)


def emit_xnT(nc, P, pre, xn_dram, stk, s_after):
    xnT = stk.enter_context(nc.sbuf_tensor(pre + "xnT", [128, NDC * S], BF16))
    s_tp = new_sem(nc, stk, pre + "tp")
    wait_all(P, "scalar", s_after)
    for dc in range(NDC):
        P.do("scalar", lambda e, dc=dc: e.dma_start(
            xnT[:, dc * S:(dc + 1) * S], xn_dram[:, dc * 128:(dc + 1) * 128], transpose=True),
            inc=(s_tp, 16))
    return xnT, s_tp


def build_l1(nc, P, stk, x1_dram, out_dram, w, pre="B_", do_ffn=True):
    from contextlib import ExitStack
    E1 = 8
    H1 = 4096
    xn_dram = nc.dram_tensor(pre + "xn", [S, D], BF16, kind="Internal").ap()

    with ExitStack() as ln_stk:
        s_xn = emit_ln(nc, P, pre + "ln", x1_dram, xn_dram, ln_stk, stk)
    xnT, s_tp = emit_xnT(nc, P, pre, xn_dram, stk, s_xn)
    tp_all = s_tp.n

    rw_bf = stk.enter_context(nc.sbuf_tensor(pre + "rw", [128, NDC * 8], BF16))
    W8 = stk.enter_context(nc.sbuf_tensor(pre + "W8", [128, NTC * 8], F32))
    rsc = stk.enter_context(nc.sbuf_tensor(pre + "rsc", [128, 40], F32))
    s_rw = new_sem(nc, stk, pre + "rw")
    s_rmm = new_sem(nc, stk, pre + "rmm")
    s_rv = new_sem(nc, stk, pre + "rv")
    s_rs2 = new_sem(nc, stk, pre + "rs2")
    s_w8 = new_sem(nc, stk, pre + "w8")

    P.do("gpsimd", lambda e: e.dma_start(
        rw_bf[:, 0:NDC * 8].rearrange("p (dc e) -> p dc e", dc=NDC),
        w["rw"].rearrange("(dc p) e -> p dc e", p=128)), inc=(s_rw, 16))

    with nc.psum_tensor(pre + "psr0", [128, 512], F32) as psr0, \
         nc.psum_tensor(pre + "psr1", [128, 512], F32) as psr1:
        ps_r = [psr0, psr1]
        P.wait("tensor", s_rw, s_rw.n)
        P.wait("tensor", s_tp, tp_all)
        for i in range(NTC):
            b = i % 2
            if i >= 2:
                P.wait("tensor", s_rs2, i - 1)
            for dc in range(NDC):
                P.do("tensor", lambda e, b=b, i=i, dc=dc: e.matmul(
                    ps_r[b][:, 0:8], xnT[:, dc * S + i * 128: dc * S + (i + 1) * 128],
                    rw_bf[:, dc * 8:(dc + 1) * 8], start=(dc == 0), stop=(dc == NDC - 1)),
                    inc=(s_rmm, 1) if dc == NDC - 1 else None)
            mm_i = s_rmm.n
            P.wait("vector", s_rmm, mm_i)
            P.do("vector", lambda e, b=b: e.tensor_reduce(rsc[:, 29:30], ps_r[b][:, 0:8],
                                                          axis=AX, op=ALU.max))
            P.drain("vector")
            P.do("vector", lambda e: e.tensor_scalar(rsc[:, 24:25], rsc[:, 29:30], -1.0, None,
                                                     op0=ALU.mult), inc=(s_rv, 1))
            P.wait("scalar", s_rv, s_rv.n)
            P.do("scalar", lambda e, b=b: e.activation(rsc[:, 0:8], ps_r[b][:, 0:8], AF.Exp,
                                                       bias=rsc[:, 24:25]), inc=(s_rs2, 1))
            P.wait("vector", s_rs2, s_rs2.n)
            P.do("vector", lambda e: e.max(out=rsc[:, 8:16], in_=rsc[:, 0:8]))
            P.drain("vector")
            P.do("vector", lambda e: e.tensor_scalar(rsc[:, 16:24], rsc[:, 0:8], rsc[:, 11:12],
                                                     None, op0=ALU.is_ge))
            P.drain("vector")
            P.do("vector", lambda e: e.tensor_tensor(rsc[:, 0:8], rsc[:, 0:8], rsc[:, 16:24],
                                                     op=ALU.mult))
            P.drain("vector")
            P.do("vector", lambda e: e.tensor_reduce(rsc[:, 27:28], rsc[:, 0:8], axis=AX,
                                                     op=ALU.add))
            P.drain("vector")
            P.do("vector", lambda e: e.reciprocal(rsc[:, 28:29], rsc[:, 27:28]))
            P.drain("vector")
            P.do("vector", lambda e, i=i: e.tensor_scalar(W8[:, i * 8:(i + 1) * 8], rsc[:, 0:8],
                                                          rsc[:, 28:29], None, op0=ALU.mult),
                 inc=(s_w8, 1))
    w8_all = s_w8.n

    s_wld = new_sem(nc, stk, pre + "wld")
    s_umm = new_sem(nc, stk, pre + "umm")
    s_uev = new_sem(nc, stk, pre + "uev")
    s_scan = new_sem(nc, stk, pre + "scan")
    s_hbf = new_sem(nc, stk, pre + "hbf")
    s_x1 = [new_sem(nc, stk, pre + f"x1_{i}") for i in range(2)]
    s_smm = new_sem(nc, stk, pre + "smm")
    s_seedv = new_sem(nc, stk, pre + "seedv")
    s_seedw = [new_sem(nc, stk, pre + f"seedw{i}") for i in range(2)]

    with nc.sbuf_tensor(pre + "BT", [128, NDC * D], BF16) as BT_sb, \
         nc.sbuf_tensor(pre + "u", [128, S], F32) as u_sb, \
         nc.sbuf_tensor(pre + "a", [128, NDC], F32) as a_sb, \
         nc.sbuf_tensor(pre + "hbf", [128, NDC * S], BF16) as hbf, \
         nc.sbuf_tensor(pre + "WT", [128, NDC * D], BF16) as WT_bf, \
         nc.sbuf_tensor(pre + "bv", [1, D], BF16) as bvec, \
         nc.sbuf_tensor(pre + "on1", [1, 128], BF16) as ones1, \
         nc.sbuf_tensor(pre + "x1t0", [128, D], F32) as x1t0, \
         nc.sbuf_tensor(pre + "x1t1", [128, D], F32) as x1t1, \
         nc.sbuf_tensor(pre + "seed0", [128, D], F32) as seed0, \
         nc.sbuf_tensor(pre + "seed1", [128, D], F32) as seed1:
        x1t = [x1t0, x1t1]
        seed = [seed0, seed1]

        P.do("vector", lambda e: e.memset(ones1[:], 1.0), inc=(s_hbf, 1))
        wait_all(P, "gpsimd", s_xn)
        for dc in range(NDC):
            P.do("gpsimd", lambda e, dc=dc: e.dma_start(
                BT_sb[:, dc * D:(dc + 1) * D], w["BT"][dc * 128:(dc + 1) * 128, :]),
                inc=(s_wld, 16))
            P.do("gpsimd", lambda e, dc=dc: e.dma_start(
                WT_bf[:, dc * D:(dc + 1) * D], w["WT"][dc * 128:(dc + 1) * 128, :]),
                inc=(s_wld, 16))
        P.do("gpsimd", lambda e: e.dma_start(bvec[:], w["b"][None, :]), inc=(s_wld, 16))
        P.do("gpsimd", lambda e: e.dma_start(a_sb[:, 0:NDC], w["A2"]), inc=(s_wld, 16))
        wld_all = s_wld.n

        with nc.psum_tensor(pre + "psu0", [128, 512], F32) as psu0, \
             nc.psum_tensor(pre + "psu1", [128, 512], F32) as psu1:
            ps_u = [psu0, psu1]
            P.wait("tensor", s_wld, wld_all)
            P.wait("tensor", s_rs2, NTC)
            P.wait("vector", s_wld, wld_all)
            for nch in range(NDC):
                for t5 in range(8):
                    b = t5 % 2
                    if nch * 8 + t5 >= 2:
                        P.wait("tensor", s_uev, nch * 8 + t5 - 1)
                    for dc in range(NDC):
                        P.do("tensor", lambda e, b=b, nch=nch, t5=t5, dc=dc: e.matmul(
                            ps_u[b][:], BT_sb[:, dc * D + nch * 128: dc * D + (nch + 1) * 128],
                            xnT[:, dc * S + t5 * 512: dc * S + (t5 + 1) * 512],
                            start=(dc == 0), stop=(dc == NDC - 1)),
                            inc=(s_umm, 1) if dc == NDC - 1 else None)
                    P.wait("scalar", s_umm, s_umm.n)
                    P.do("scalar", lambda e, b=b, t5=t5: e.activation(
                        u_sb[:, t5 * 512:(t5 + 1) * 512], ps_u[b][:], AF.Copy), inc=(s_uev, 1))
                P.wait("vector", s_uev, s_uev.n)
                P.do("vector", lambda e, nch=nch: e.tensor_tensor_scan(
                    hbf[:, nch * S:(nch + 1) * S],
                    a_sb[:, nch:nch + 1].to_broadcast([128, S]), u_sb[:], 0.0,
                    op0=ALU.mult, op1=ALU.add), inc=(s_scan, 1))
                P.wait("scalar", s_scan, s_scan.n)
        scan_all = s_scan.n

        with nc.psum_tensor(pre + "pss0", [128, D], F32) as pss0, \
             nc.psum_tensor(pre + "pss1", [128, D], F32) as pss1:
            ps_s = [pss0, pss1]
            P.wait("tensor", s_hbf, 1)
            P.wait("tensor", s_scan, scan_all)
            for i in range(NTC):
                b = i % 2
                if i >= 2:
                    P.wait("sync", s_seedv, 2 * (i - 1))
                P.do("sync", lambda e, b=b, i=i: e.dma_start(
                    x1t[b][:], x1_dram[i * 128:(i + 1) * 128, :]), inc=(s_x1[b], 16))
                x1_i = s_x1[b].n
                if i >= 2:
                    P.wait("tensor", s_seedv, 2 * (i - 1))
                for half in range(2):
                    for nch in range(NDC):
                        P.do("tensor", lambda e, b=b, i=i, half=half, nch=nch: e.matmul(
                            ps_s[b][:, half * 512:(half + 1) * 512],
                            hbf[:, nch * S + i * 128: nch * S + (i + 1) * 128],
                            WT_bf[:, nch * D + half * 512: nch * D + (half + 1) * 512],
                            start=(nch == 0), stop=False))
                    if True:
                        P.do("tensor", lambda e, b=b, half=half: e.matmul(
                            ps_s[b][:, half * 512:(half + 1) * 512], ones1[:],
                            bvec[:, half * 512:(half + 1) * 512], start=False, stop=True),
                            inc=(s_smm, 1))
                    else:
                        P.do("tensor", lambda e, b=b, half=half: e.matmul(
                            ps_s[b][:, half * 512:(half + 1) * 512], ones1[:],
                            bvec[:, half * 512:(half + 1) * 512], start=False, stop=True),
                            inc=(s_smm, 1))
                mm_i = s_smm.n
                P.wait("vector", s_smm, mm_i)
                P.wait("vector", s_x1[b], x1_i)
                if i >= 2:
                    P.wait("vector", s_seedw[b], ((i - 2) // 2 + 1) * 16)
                P.do("vector", lambda e, b=b: e.tensor_tensor(
                    seed[b][:, 0:512], ps_s[b][:, 0:512], x1t[b][:, 0:512], op=ALU.add),
                    inc=(s_seedv, 1))
                P.do("vector", lambda e, b=b: e.tensor_tensor(
                    seed[b][:, 512:1024], ps_s[b][:, 512:1024], x1t[b][:, 512:1024], op=ALU.add),
                    inc=(s_seedv, 1))
                P.wait("gpsimd", s_seedv, s_seedv.n)
                P.do("gpsimd", lambda e, b=b, i=i: e.dma_start(
                    out_dram[i * 128:(i + 1) * 128, :], seed[b][:]), inc=(s_seedw[b], 16))
        smm_all = s_smm.n

    if not do_ffn:
        wait_all(P, "gpsimd", s_seedw)
        return s_seedw
    TT = 512
    NTT = S // TT
    FQ = 1024
    NQ = H1 // FQ
    NFC = FQ // 128
    up_sb = [stk.enter_context(nc.sbuf_tensor(f"{pre}up{i}", [128, NDC * FQ], BF16))
             for i in range(2)]
    dn_sb = [stk.enter_context(nc.sbuf_tensor(f"{pre}dn{i}", [128, NFC * D], BF16))
             for i in range(2)]
    hT = stk.enter_context(nc.sbuf_tensor(pre + "hT", [128, NFC * TT], BF16))
    stage = [stk.enter_context(nc.sbuf_tensor(f"{pre}stg{i}", [128, 4 * D], F32))
             for i in range(2)]
    s_wq = [new_sem(nc, stk, pre + f"wq{i}") for i in range(2)]
    s_qfree = new_sem(nc, stk, pre + "qfree")
    s_hmm = new_sem(nc, stk, pre + "hmm")
    s_gel = new_sem(nc, stk, pre + "gel")
    s_omm = new_sem(nc, stk, pre + "omm")
    s_stg = new_sem(nc, stk, pre + "stg")
    s_acc = [new_sem(nc, stk, pre + f"acc{i}") for i in range(2)]

    def load_quarter(qi, e_, q_):
        buf = qi % 2
        if qi >= 2:
            P.wait("gpsimd", s_qfree, qi - 1)
        P.do("gpsimd", lambda e, e_=e_, q_=q_, buf=buf: e.dma_start(
            up_sb[buf][:].rearrange("p (dc f) -> p dc f", dc=NDC),
            w["up"][e_].rearrange("(dc p) f -> p dc f", p=128)[:, :, q_ * FQ:(q_ + 1) * FQ]),
            inc=(s_wq[buf], 16))
        P.do("gpsimd", lambda e, e_=e_, q_=q_, buf=buf: e.dma_start(
            dn_sb[buf][:].rearrange("p (fc d) -> p fc d", fc=NFC),
            w["dn"][e_][q_ * FQ:(q_ + 1) * FQ, :].rearrange("(fc p) d -> p fc d", p=128)),
            inc=(s_wq[buf], 16))

    quarters = [(e_, q_) for e_ in range(E1) for q_ in range(NQ)]
    P.wait("scalar", s_w8, w8_all)
    wait_all(P, "gpsimd", s_seedw)
    P.wait("gpsimd", s_smm, smm_all)
    load_quarter(0, *quarters[0])
    load_quarter(1, *quarters[1])
    with nc.psum_tensor(pre + "psh0", [128, TT], F32) as psh0, \
         nc.psum_tensor(pre + "psh1", [128, TT], F32) as psh1, \
         nc.psum_tensor(pre + "pso0", [128, D], F32) as pso0, \
         nc.psum_tensor(pre + "pso1", [128, D], F32) as pso1:
        ps_h = [psh0, psh1]
        ps_o = [pso0, pso1]
        for qi, (e_, q_) in enumerate(quarters):
            buf = qi % 2
            P.wait("tensor", s_wq[buf], (qi // 2 + 1) * 32)
            for tt in range(NTT):
                gi = qi * NTT + tt
                for fc in range(NFC):
                    hb = fc % 2
                    gidx = gi * NFC + fc
                    if gidx >= 2:
                        P.wait("tensor", s_gel, gidx - 1)
                    for dc in range(NDC):
                        P.do("tensor", lambda e, buf=buf, fc=fc, dc=dc, tt=tt, hb=hb: e.matmul(
                            ps_h[hb][:],
                            up_sb[buf][:, dc * FQ + fc * 128: dc * FQ + (fc + 1) * 128],
                            xnT[:, dc * S + tt * TT: dc * S + (tt + 1) * TT],
                            start=(dc == 0), stop=(dc == NDC - 1)),
                            inc=(s_hmm, 1) if dc == NDC - 1 else None)
                    P.wait("scalar", s_hmm, s_hmm.n)
                    if gi >= 1 and fc == 0:
                        P.wait("scalar", s_omm, gi * 4)
                    P.do("scalar", lambda e, fc=fc, hb=hb: e.activation(
                        hT[:, fc * TT:(fc + 1) * TT], ps_h[hb][:], AF.Gelu_apprx_tanh),
                        inc=(s_gel, 1))
                gel_i = s_gel.n
                P.wait("tensor", s_gel, gel_i)
                for tc4 in range(4):
                    ob = tc4 % 2
                    oidx = gi * 4 + tc4
                    if oidx >= 2:
                        P.wait("tensor", s_stg, oidx - 1)
                    for fc in range(NFC):
                        for half in range(2):
                            P.do("tensor",
                                 lambda e, buf=buf, fc=fc, tc4=tc4, half=half, ob=ob: e.matmul(
                                     ps_o[ob][:, half * 512:(half + 1) * 512],
                                     hT[:, fc * TT + tc4 * 128: fc * TT + (tc4 + 1) * 128],
                                     dn_sb[buf][:, fc * D + half * 512: fc * D + (half + 1) * 512],
                                     start=(fc == 0), stop=(fc == NFC - 1)),
                                 inc=(s_omm, 1) if (fc == NFC - 1 and half == 1) else None)
                    omm_i = s_omm.n
                    P.wait("scalar", s_omm, omm_i)
                    sb = gi % 2
                    if gi >= 2 and tc4 == 0:
                        P.wait("scalar", s_acc[sb], ((gi - 2) // 2 + 1) * 16)
                    P.do("scalar", lambda e, tc4=tc4, ob=ob, tt=tt, e_=e_, sb=sb: e.activation(
                        stage[sb][:, tc4 * D:(tc4 + 1) * D], ps_o[ob][:], AF.Copy,
                        scale=W8[:, (tt * 4 + tc4) * 8 + e_: (tt * 4 + tc4) * 8 + e_ + 1]),
                        inc=(s_stg, 1))
                stg_i = s_stg.n
                P.wait("gpsimd", s_stg, stg_i)
                P.do("gpsimd", lambda e, tt=tt, sb=(gi % 2): e.dma_start(
                    out_dram[tt * TT:(tt + 1) * TT, :].rearrange("(c p) d -> p c d", p=128),
                    stage[sb][:].rearrange("p (c d) -> p c d", c=4), accum_op=ALU.add),
                    inc=(s_acc[gi % 2], 16))
            P.do("tensor", lambda e: e.nop(), inc=(s_qfree, 1))
            if qi + 2 < len(quarters):
                load_quarter(qi + 2, *quarters[qi + 2])

    wait_all(P, "gpsimd", s_acc)
    return s_acc



NH = 16
DK = 64
E0 = 16
CAP = 256
SINK_IT = 6
NQT = S // 512
NKC = S // 128


def host_bucket(d):
    import math
    if d > 0:
        ret, ad = 16, d
    else:
        ret, ad = 0, -d
    if ad < 8:
        return ret + ad
    v = 8 + int(math.log(ad / 8) / math.log(128 / 8) * 8)
    return ret + min(v, 15)


def host_consts():
    oh = np.zeros((32, 8191), np.float32)
    for e in range(8191):
        oh[host_bucket(4095 - e), e] = 1.0
    J = np.zeros((128, 128), np.float32)
    for p in range(128):
        J[p, 127 - p] = 1.0
    ident = np.eye(128, dtype=np.float32)
    return {
        "oneh": oh.astype(ml_dtypes.bfloat16),
        "J": J.astype(ml_dtypes.bfloat16),
        "identf": ident,
        "identb": ident.astype(ml_dtypes.bfloat16),
    }


def band_kind(qt, kc):
    if 128 * kc + 127 - 512 * qt <= -91:
        return 0
    if 128 * kc - (512 * qt + 511) >= 91:
        return 1
    return 2


def build_l0(nc, P, stk, x_dram, x1_dram, w, pre="A_", stage="full", dump_cb=None):
    from contextlib import ExitStack
    l0t = ExitStack()
    xn_dram = nc.dram_tensor(pre + "xn", [S, D], BF16, kind="Internal").ap()
    wbias_dram = nc.dram_tensor(pre + "wb", [NH, 8191], BF16, kind="Internal").ap()

    sc = l0t.enter_context(nc.sbuf_tensor(pre + "sc", [128, 32], F32))
    sc2 = l0t.enter_context(nc.sbuf_tensor(pre + "sc2", [128, 32], F32))
    cb = l0t.enter_context(nc.sbuf_tensor(pre + "cb", [128, 3 * NH], F32))
    onesf = l0t.enter_context(nc.sbuf_tensor(pre + "onesf", [128, 128], F32))
    ones_bf = l0t.enter_context(nc.sbuf_tensor(pre + "onesb", [128, 128], BF16))
    J_sb = l0t.enter_context(nc.sbuf_tensor(pre + "J", [128, 128], BF16))
    idf_sb = l0t.enter_context(nc.sbuf_tensor(pre + "idf", [128, 128], F32))
    idb_sb = l0t.enter_context(nc.sbuf_tensor(pre + "idb", [128, 128], BF16))
    trow_l = l0t.enter_context(nc.sbuf_tensor(pre + "trl", [1, 16], F32))
    trow_r = l0t.enter_context(nc.sbuf_tensor(pre + "trr", [1, 16], F32))
    tabf = l0t.enter_context(nc.sbuf_tensor(pre + "tabf", [1, 512], F32))
    G_bf = l0t.enter_context(nc.sbuf_tensor(pre + "G", [16, S], BF16))
    gatesT = l0t.enter_context(nc.sbuf_tensor(pre + "gatesT", [128, 2 * E0], F32))
    idxT = l0t.enter_context(nc.sbuf_tensor(pre + "idxT", [128, 2 * E0], I32))
    s_cst = new_sem(nc, stk, pre + "cst")
    s_ones = new_sem(nc, stk, pre + "ones")

    P.do("gpsimd", lambda e: e.dma_start(trow_l[:], w["table"][15:16, :]), inc=(s_cst, 16))
    P.do("gpsimd", lambda e: e.dma_start(trow_r[:], w["table"][31:32, :]), inc=(s_cst, 16))
    P.do("gpsimd", lambda e: e.dma_start(tabf[:], w["tabflat"][None, :]), inc=(s_cst, 16))
    P.do("gpsimd", lambda e: e.dma_start(J_sb[:], w["J"]), inc=(s_cst, 16))
    P.do("gpsimd", lambda e: e.dma_start(idf_sb[:], w["identf"]), inc=(s_cst, 16))
    P.do("gpsimd", lambda e: e.dma_start(idb_sb[:], w["identb"]), inc=(s_cst, 16))
    cst_all = s_cst.n
    P.do("vector", lambda e: e.memset(onesf[:], 1.0))
    P.do("vector", lambda e: e.memset(ones_bf[:], 1.0), inc=(s_ones, 1))
    P.wait("vector", s_cst, cst_all)
    P.do("vector", lambda e: e.tensor_reduce(sc[0:1, 31:32], tabf[:], axis=AX, op=ALU.max,
                                             apply_absolute_value=True), inc=(s_ones, 1))

    with ExitStack() as ln_stk:
        s_xn = emit_ln(nc, P, pre + "ln", x_dram, xn_dram, ln_stk, stk)
    mid_stk = l0t.enter_context(ExitStack())
    xnT, s_tp = emit_xnT(nc, P, pre, xn_dram, mid_stk, s_xn)
    tp_all = s_tp.n

    s_wbm = new_sem(nc, stk, pre + "wbm")
    s_wbe = new_sem(nc, stk, pre + "wbe")
    s_wbw = new_sem(nc, stk, pre + "wbw")
    with nc.sbuf_tensor(pre + "tabb", [32, 16], BF16) as tab_bf, \
         nc.sbuf_tensor(pre + "oneh", [32, 8191], BF16) as oneh_sb, \
         nc.sbuf_tensor(pre + "wrow", [16, 8191], BF16) as wrow, \
         nc.psum_tensor(pre + "psw0", [128, 512], F32) as psw0, \
         nc.psum_tensor(pre + "psw1", [128, 512], F32) as psw1:
        psw = [psw0, psw1]
        s_wbl = new_sem(nc, stk, pre + "wbl")
        wait_all(P, "gpsimd", s_xn)
        P.do("gpsimd", lambda e: e.dma_start(tab_bf[:], w["table"]), inc=(s_wbl, 16))
        P.do("gpsimd", lambda e: e.dma_start(oneh_sb[:], w["oneh"]), inc=(s_wbl, 16))
        P.wait("tensor", s_wbl, s_wbl.n)
        for t in range(16):
            n = 511 if t == 15 else 512
            pb = t % 2
            if t >= 2:
                P.wait("tensor", s_wbe, t - 1)
            P.do("tensor", lambda e, t=t, n=n, pb=pb: e.matmul(
                psw[pb][0:16, 0:n], tab_bf[:], oneh_sb[:, t * 512:t * 512 + n],
                start=True, stop=True), inc=(s_wbm, 1))
            P.wait("scalar", s_wbm, s_wbm.n)
            P.do("scalar", lambda e, t=t, n=n, pb=pb: e.activation(
                wrow[:, t * 512:t * 512 + n], psw[pb][0:16, 0:n], AF.Copy), inc=(s_wbe, 1))
        P.wait("sync", s_wbe, s_wbe.n)
        P.do("sync", lambda e: e.dma_start(wbias_dram, wrow[:]), inc=(s_wbw, 16))
    wbw_all = s_wbw.n



    s_gql = new_sem(nc, stk, pre + "gql")
    s_gmm = new_sem(nc, stk, pre + "gmm")
    s_gev = new_sem(nc, stk, pre + "gev")
    s_sks = new_sem(nc, stk, pre + "sks")
    s_skv = new_sem(nc, stk, pre + "skv")
    s_skt = new_sem(nc, stk, pre + "skt")
    s_tkv = new_sem(nc, stk, pre + "tkv")
    s_gdone = new_sem(nc, stk, pre + "gdone")

    with nc.sbuf_tensor(pre + "gqT", [128, NDC * E0], BF16) as gqT_bf, \
         nc.sbuf_tensor(pre + "tt", [128, NTC * E0], F32) as tt_sb, \
         nc.sbuf_tensor(pre + "ex", [128, NTC * E0], F32) as ex_sb, \
         nc.sbuf_tensor(pre + "lsn", [128, NTC], F32) as lsn_sb, \
         nc.sbuf_tensor(pre + "aff", [16, S], F32) as aff_sb, \
         nc.sbuf_tensor(pre + "affw", [16, S], F32) as affw_sb, \
         nc.sbuf_tensor(pre + "m8g", [16, 8], F32) as m8g, \
         nc.sbuf_tensor(pre + "gates", [16, CAP], F32) as gates_sb, \
         nc.sbuf_tensor(pre + "gidx", [16, CAP], U32) as gidx_sb, \
         nc.sbuf_tensor(pre + "lt", [1, 2], F32) as lt_sb, \
         nc.psum_tensor(pre + "psg0", [128, 512], F32) as psg0, \
         nc.psum_tensor(pre + "psg1", [128, 512], F32) as psg1:
        ps_g = [psg0, psg1]
        P.wait("gpsimd", s_wbw, wbw_all)
        P.do("gpsimd", lambda e: e.dma_start(
            gqT_bf[:].rearrange("p (dc e) -> p dc e", dc=NDC),
            w["gqT"].rearrange("(dc p) e -> p dc e", p=128)), inc=(s_gql, 16))
        P.do("gpsimd", lambda e: e.dma_start(lt_sb[:, 0:1], w["ltemp"][None, :]), inc=(s_gql, 16))
        gq_i = s_gql.n
        P.wait("scalar", s_gql, gq_i)
        P.do("scalar", lambda e: e.activation(lt_sb[:, 1:2], lt_sb[:, 0:1], AF.Exp, scale=-1.0),
             inc=(s_sks, 1))
        P.wait("tensor", s_sks, s_sks.n)
        P.wait("tensor", s_ones, 1)
        P.do("tensor", lambda e: e.matmul(psg0[0:128, 0:1], onesf[0:1, 0:128], lt_sb[0:1, 1:2],
                                          start=True, stop=True), inc=(s_skt, 1))
        P.wait("scalar", s_skt, s_skt.n)
        P.do("scalar", lambda e: e.activation(sc2[:, 0:1], psg0[0:128, 0:1], AF.Copy),
             inc=(s_sks, 1))
        itp_ready = s_sks.n

        P.wait("tensor", s_tp, tp_all)
        P.wait("tensor", s_gql, gq_i)
        P.wait("tensor", s_sks, itp_ready)
        for c in range(NTC):
            bb = c % 2
            if c >= 2:
                P.wait("tensor", s_gev, c - 1)
            for dc in range(NDC):
                P.do("tensor", lambda e, c=c, dc=dc, bb=bb: e.matmul(
                    ps_g[bb][:, 0:E0], xnT[:, dc * S + c * 128:dc * S + (c + 1) * 128],
                    gqT_bf[:, dc * E0:(dc + 1) * E0], start=(dc == 0), stop=(dc == NDC - 1)),
                    inc=(s_gmm, 1) if dc == NDC - 1 else None)
            P.wait("scalar", s_gmm, s_gmm.n)
            if c == 0:
                P.wait("scalar", s_sks, itp_ready)
            P.do("scalar", lambda e, c=c, bb=bb: e.activation(
                tt_sb[:, c * E0:(c + 1) * E0], ps_g[bb][:, 0:E0], AF.Copy,
                scale=sc2[:, 0:1]), inc=(s_gev, 1))
        gev_all = s_gev.n

        P.wait("scalar", s_gev, gev_all)
        tt3 = tt_sb[:].rearrange("p (c e) -> p c e", c=NTC)
        ex3 = ex_sb[:].rearrange("p (c e) -> p c e", c=NTC)
        for it in range(SINK_IT):
            P.do("scalar", lambda e: e.activation(ex_sb[:], tt_sb[:], AF.Exp), inc=(s_sks, 1))
            P.wait("vector", s_sks, s_sks.n)
            P.do("vector", lambda e: e.tensor_reduce(lsn_sb[:], ex3, axis=AX, op=ALU.add),
                 inc=(s_skv, 1))
            P.wait("scalar", s_skv, s_skv.n)
            P.do("scalar", lambda e: e.activation(lsn_sb[:], lsn_sb[:], AF.Ln), inc=(s_sks, 1))
            P.wait("vector", s_sks, s_sks.n)
            P.do("vector", lambda e: e.tensor_tensor(
                tt3, tt3, lsn_sb[:].unsqueeze(2).to_broadcast([128, NTC, E0]),
                op=ALU.subtract), inc=(s_skv, 1))
            P.wait("scalar", s_skv, s_skv.n)
            P.do("scalar", lambda e: e.activation(ex_sb[:], tt_sb[:], AF.Exp), inc=(s_sks, 1))
            P.wait("tensor", s_sks, s_sks.n)
            for c in range(NTC):
                P.do("tensor", lambda e, c=c: e.matmul(
                    psg1[0:1, 0:E0], onesf[:, 0:1], ex_sb[:, c * E0:(c + 1) * E0],
                    start=(c == 0), stop=(c == NTC - 1)),
                    inc=(s_skt, 1) if c == NTC - 1 else None)
            P.wait("scalar", s_skt, s_skt.n)
            P.do("scalar", lambda e: e.activation(sc2[0:1, 1:17], psg1[0:1, 0:E0], AF.Ln),
                 inc=(s_sks, 1))
            P.wait("tensor", s_sks, s_sks.n)
            P.do("tensor", lambda e: e.matmul(psg1[0:128, 0:E0], onesf[0:1, 0:128],
                                              sc2[0:1, 1:17], start=True, stop=True),
                 inc=(s_skt, 1))
            P.wait("vector", s_skt, s_skt.n)
            P.do("vector", lambda e: e.tensor_tensor(
                tt3, tt3,
                psg1[0:128, 0:E0].unsqueeze(1).to_broadcast([128, NTC, E0]),
                op=ALU.subtract), inc=(s_skv, 1))
            P.wait("scalar", s_skv, s_skv.n)
        P.do("scalar", lambda e: e.activation(ex_sb[:], tt_sb[:], AF.Exp), inc=(s_sks, 1))
        aff_ready = s_sks.n

        P.wait("tensor", s_sks, aff_ready)
        for c in range(NTC):
            bb = c % 2
            if c >= 2:
                P.wait("tensor", s_skv, s_skv.n - 1)
            P.do("tensor", lambda e, c=c, bb=bb: e.transpose(
                ps_g[bb][0:16, 0:128], ex_sb[:, c * E0:(c + 1) * E0], idf_sb[:]),
                inc=(s_skt, 1))
            P.wait("vector", s_skt, s_skt.n)
            P.do("vector", lambda e, c=c, bb=bb: e.tensor_copy(
                aff_sb[:, c * 128:(c + 1) * 128], ps_g[bb][0:16, 0:128]), inc=(s_skv, 1))
        affT_done = s_skv.n

        P.wait("vector", s_skv, affT_done)
        src = aff_sb
        for r in range(CAP // 8):
            P.do("vector", lambda e, src=src: e.max(out=m8g[:], in_=src[:]))
            P.drain("vector")
            P.do("vector", lambda e, r=r, src=src: e.tensor_copy(
                gates_sb[:, r * 8:(r + 1) * 8], m8g[:]))
            P.do("vector", lambda e, r=r, src=src: e.max_index(
                out=gidx_sb[:, r * 8:(r + 1) * 8], in_max=m8g[:], in_values=src[:]))
            P.drain("vector")
            P.do("vector", lambda e, src=src: e.match_replace(
                out=affw_sb[:], in_to_replace=m8g[:], in_values=src[:], imm_value=0.0),
                inc=(s_tkv, 1))
            P.drain("vector")
            src = affw_sb
        P.do("vector", lambda e: e.tensor_tensor(G_bf[:], aff_sb[:], affw_sb[:],
                                                 op=ALU.subtract), inc=(s_tkv, 1))
        P.drain("vector")
        P.do("vector", lambda e: e.tensor_copy(affw_sb[:, 0:CAP], gidx_sb[:]), inc=(s_tkv, 1))
        P.wait("tensor", s_tkv, s_tkv.n)
        for cc in range(2):
            P.do("tensor", lambda e, cc=cc: e.transpose(
                psg0[0:128, 0:16], gates_sb[:, cc * 128:(cc + 1) * 128], idf_sb[0:16, 0:16]),
                inc=(s_skt, 1))
            P.wait("vector", s_skt, s_skt.n)
            P.do("vector", lambda e, cc=cc: e.tensor_copy(
                gatesT[:, cc * 16:(cc + 1) * 16], psg0[0:128, 0:16]), inc=(s_skv, 1))
            P.wait("tensor", s_skv, s_skv.n)
            P.do("tensor", lambda e, cc=cc: e.transpose(
                psg1[0:128, 0:16], affw_sb[:, cc * 128:(cc + 1) * 128], idf_sb[0:16, 0:16]),
                inc=(s_skt, 1))
            P.wait("vector", s_skt, s_skt.n)
            P.do("vector", lambda e, cc=cc: e.tensor_copy(
                idxT[:, cc * 16:(cc + 1) * 16], psg1[0:128, 0:16]), inc=(s_skv, 1))
        P.do("vector", lambda e: e.tensor_copy(sc2[:, 20:21], sc2[:, 0:1]), inc=(s_gdone, 1))
    gdone = s_gdone.n
    if dump_cb is not None:
        dump_cb("gating", dict(G_bf=G_bf, gatesT=gatesT, idxT=idxT, s_gdone=s_gdone, cb=cb, sc=sc))
    if stage == "gating":
        return s_gdone

    from contextlib import ExitStack as _ES
    attnT = mid_stk.enter_context(nc.sbuf_tensor(pre + "attnT", [128, NDC * S], BF16))
    s_wh = [new_sem(nc, stk, pre + f"wh{i}") for i in range(2)]
    s_qkmm = new_sem(nc, stk, pre + "qkmm")
    s_qkev = new_sem(nc, stk, pre + "qkev")
    s_vmm = new_sem(nc, stk, pre + "vmm")
    s_vev = new_sem(nc, stk, pre + "vev")
    s_nsq = new_sem(nc, stk, pre + "nsq")
    s_nmm = new_sem(nc, stk, pre + "nmm")
    s_nrd = new_sem(nc, stk, pre + "nrd")
    s_cbv = new_sem(nc, stk, pre + "cbv")
    s_cbb = new_sem(nc, stk, pre + "cbb")
    s_cbe = new_sem(nc, stk, pre + "cbe")
    s_smm = new_sem(nc, stk, pre + "smm")
    s_pex = new_sem(nc, stk, pre + "pex")
    s_pvk = new_sem(nc, stk, pre + "pvk")
    s_bmm = new_sem(nc, stk, pre + "bmm")
    s_bnd = [new_sem(nc, stk, pre + f"bnd{i}") for i in range(2)]
    s_on = new_sem(nc, stk, pre + "on")
    s_bc = new_sem(nc, stk, pre + "bc")
    s_oev = new_sem(nc, stk, pre + "oev")
    s_att = new_sem(nc, stk, pre + "att")
    s_hfin = new_sem(nc, stk, pre + "hfin")

    with _ES() as at_stk:
        _ac = at_stk.enter_context
        wq_sb = [_ac(nc.sbuf_tensor(f"{pre}wq{i}", [128, NDC * DK], BF16)) for i in range(2)]
        wk_sb = [_ac(nc.sbuf_tensor(f"{pre}wk{i}", [128, NDC * DK], BF16)) for i in range(2)]
        wv_sb = [_ac(nc.sbuf_tensor(f"{pre}wv{i}", [128, NDC * DK], BF16)) for i in range(2)]
        qt_sb = [_ac(nc.sbuf_tensor(f"{pre}qt{i}", [128, S], BF16)) for i in range(2)]
        kt_sb = [_ac(nc.sbuf_tensor(f"{pre}kt{i}", [128, S], BF16)) for i in range(2)]
        v_sb = [_ac(nc.sbuf_tensor(f"{pre}v{i}", [128, NKC * 72], BF16)) for i in range(2)]
        sqsm = [_ac(nc.sbuf_tensor(f"{pre}sqsm{i}", [128, 512], BF16)) for i in range(2)]
        pt_sb = [_ac(nc.sbuf_tensor(f"{pre}pt{i}", [128, 1024], BF16)) for i in range(3)]
        b3_sb = [_ac(nc.sbuf_tensor(f"{pre}b3{i}", [128, 512], BF16)) for i in range(2)]
        rcp_s = _ac(nc.sbuf_tensor(pre + "rcps", [128, 1024], F32))
        otmp = _ac(nc.sbuf_tensor(pre + "otmp", [128, 512], BF16))
        psq = [_ac(nc.psum_tensor(f"{pre}psq{i}", [128, 512], F32)) for i in range(2)]
        ps_sc = [_ac(nc.psum_tensor(f"{pre}pssc{i}", [128, 1024], F32)) for i in range(2)]
        pso = _ac(nc.psum_tensor(pre + "pso", [128, 512], F32))
        psb2 = _ac(nc.psum_tensor(pre + "psb2", [128, 512], F32))
        v0, v1 = v_sb

        P.do("vector", lambda e: e.memset(v0[:], 1.0))
        P.do("vector", lambda e: e.memset(v1[:], 1.0), inc=(s_vev, 1))
        band_ctr = [0]
        band_g = []
        seg_ctr = [0]
        seg_pvk_end = []
        P.wait("gpsimd", s_gdone, gdone)

        for h in range(NH):
            buf = h % 2
            if h >= 2:
                P.wait("gpsimd", s_hfin, h - 1)
            for (dst, key) in ((wq_sb[buf], "wq"), (wk_sb[buf], "wk"), (wv_sb[buf], "wv")):
                P.do("gpsimd", lambda e, dst=dst, key=key, h=h: e.dma_start(
                    dst[:].rearrange("p (dc i) -> p dc i", dc=NDC),
                    w[key].rearrange("(dc p) i -> p dc i", p=128)[:, :, h * DK:(h + 1) * DK]),
                    inc=(s_wh[buf], 16))
            wh_i = s_wh[buf].n
            P.wait("tensor", s_wh[buf], wh_i)
            if h >= 2:
                P.wait("tensor", s_pvk, (h - 1) * 8 * NKC)
            P.wait("tensor", s_nrd, s_nrd.n)
            for mat, dst in ((wq_sb[buf], qt_sb[buf]), (wk_sb[buf], kt_sb[buf])):
                for t in range(NQT):
                    pb = t % 2
                    P.wait("tensor", s_qkev, max(0, s_qkev.n - 1))
                    for dc in range(NDC):
                        P.do("tensor", lambda e, mat=mat, t=t, dc=dc, pb=pb: e.matmul(
                            psq[pb][0:DK, :], mat[:, dc * DK:(dc + 1) * DK],
                            xnT[:, dc * S + t * 512:dc * S + (t + 1) * 512],
                            start=(dc == 0), stop=(dc == NDC - 1)),
                            inc=(s_qkmm, 1) if dc == NDC - 1 else None)
                    P.wait("vector", s_qkmm, s_qkmm.n)
                    if h >= 2 and mat is wq_sb[buf] and t == 0:
                        P.wait("vector", s_pvk, (h - 1) * 8 * NKC)
                    P.do("vector", lambda e, dst=dst, t=t, pb=pb: e.tensor_copy(
                        dst[0:DK, t * 512:(t + 1) * 512], psq[pb][0:DK, :]),
                        inc=(s_qkev, 1))
            P.wait("tensor", s_qkev, s_qkev.n)
            for c in range(NKC):
                vb = c % 2
                P.wait("tensor", s_vev, max(1, s_vev.n - 1))
                for dc in range(NDC):
                    P.do("tensor", lambda e, buf=buf, c=c, dc=dc, vb=vb: e.matmul(
                        psq[vb][:, 0:DK], xnT[:, dc * S + c * 128:dc * S + (c + 1) * 128],
                        wv_sb[buf][:, dc * DK:(dc + 1) * DK],
                        start=(dc == 0), stop=(dc == NDC - 1)),
                        inc=(s_vmm, 1) if dc == NDC - 1 else None)
                P.wait("vector", s_vmm, s_vmm.n)
                P.do("vector", lambda e, buf=buf, c=c, vb=vb: e.tensor_copy(
                    v_sb[buf][:, c * 72:c * 72 + DK], psq[vb][:, 0:DK]),
                    inc=(s_vev, 1))
            P.wait("tensor", s_vev, s_vev.n)
            for widx, src in ((0, qt_sb[buf]), (1, kt_sb[buf])):
                P.wait("scalar", s_qkev, s_qkev.n)
                for t in range(NQT):
                    pb = t % 2
                    P.wait("scalar", s_nmm, max(0, s_nmm.n - 1))
                    P.do("scalar", lambda e, src=src, t=t, pb=pb: e.activation(
                        sqsm[pb][0:DK, :], src[0:DK, t * 512:(t + 1) * 512],
                        AF.Square), inc=(s_nsq, 1))
                    P.wait("tensor", s_nsq, s_nsq.n)
                    if s_nrd.n >= 1:
                        P.wait("tensor", s_nrd, max(0, s_nrd.n - 1))
                    P.do("tensor", lambda e, t=t, pb=pb: e.matmul(
                        psq[pb][0:1, :], ones_bf[0:DK, 0:1], sqsm[pb][0:DK, :],
                        start=True, stop=True), inc=(s_nmm, 1))
                    P.wait("vector", s_nmm, s_nmm.n)
                    P.do("vector", lambda e, t=t, pb=pb: e.tensor_reduce(
                        sc[0:1, t:t + 1], psq[pb][0:1, :], axis=AX, op=ALU.max),
                        inc=(s_nrd, 1))
                P.wait("vector", s_nrd, s_nrd.n)
                P.drain("vector")
                P.do("vector", lambda e, widx=widx: e.tensor_reduce(
                    sc[0:1, 8 + widx:9 + widx], sc[0:1, 0:NQT], axis=AX, op=ALU.max),
                    inc=(s_nrd, 1))
                P.drain("vector")
            P.do("vector", lambda e: e.tensor_scalar(sc[0:1, 10:11], sc[0:1, 8:9], 0.5, None,
                                                     op0=ALU.mult))
            P.drain("vector")
            P.do("vector", lambda e: e.scalar_tensor_tensor(sc[0:1, 11:12], sc[0:1, 9:10], 0.5,
                                                            sc[0:1, 10:11], op0=ALU.mult,
                                                            op1=ALU.add))
            P.drain("vector")
            P.do("vector", lambda e: e.tensor_tensor(sc[0:1, 12:13], sc[0:1, 11:12],
                                                     sc[0:1, 31:32], op=ALU.add))
            P.drain("vector")
            P.do("vector", lambda e: e.tensor_scalar(sc[0:1, 13:14], sc[0:1, 12:13], -1.0, None,
                                                     op0=ALU.mult))
            P.drain("vector")
            P.do("vector", lambda e, h=h: e.tensor_tensor(sc[0:1, 14:15], trow_l[0:1, h:h + 1],
                                                          sc[0:1, 13:14], op=ALU.add))
            P.do("vector", lambda e, h=h: e.tensor_tensor(sc[0:1, 15:16], trow_r[0:1, h:h + 1],
                                                          sc[0:1, 13:14], op=ALU.add),
                 inc=(s_cbv, 1))
            P.drain("vector")
            cbv_i = s_cbv.n
            P.wait("tensor", s_cbv, cbv_i)
            P.wait("tensor", s_att, s_att.n)
            for j, col in ((0, 14), (1, 15), (2, 13)):
                P.wait("tensor", s_cbe, s_cbe.n)
                P.do("tensor", lambda e, col=col: e.matmul(
                    psb2[:, 0:1], onesf[0:1, 0:128], sc[0:1, col:col + 1],
                    start=True, stop=True), inc=(s_cbb, 1))
                P.wait("vector", s_cbb, s_cbb.n)
                P.do("vector", lambda e, h=h, j=j: e.tensor_copy(
                    cb[:, 3 * h + j:3 * h + j + 1], psb2[:, 0:1]), inc=(s_cbe, 1))
            cbe_i = s_cbe.n
            P.wait("tensor", s_qkev, s_qkev.n)
            P.wait("tensor", s_vev, s_vev.n)
            P.wait("scalar", s_cbe, cbe_i)
            for qt_ in range(NQT):
                if h + qt_ > 0:
                    P.wait("tensor", s_oev, s_oev.n)
                    P.wait("tensor", s_on, s_on.n)
                kinds = [band_kind(qt_, kc) for kc in range(NKC)]
                segs = []
                kc = 0
                while kc < NKC:
                    if kc + 1 < NKC and kinds[kc + 1] == kinds[kc]:
                        segs.append((kc, 2, kinds[kc]))
                        kc += 2
                    else:
                        segs.append((kc, 1, kinds[kc]))
                        kc += 1
                def emit_pv(ent):
                    (slot_, kc0_, w2_, pex_t, vbuf) = ent
                    P.wait("tensor", s_pex, pex_t)
                    for ki in range(w2_):
                        kc = kc0_ + ki
                        P.do("tensor", lambda e, kc=kc, slot_=slot_, ki=ki, vbuf=vbuf: e.matmul(
                            pso[0:65, :], v_sb[vbuf][:, kc * 72:kc * 72 + 65],
                            pt_sb[slot_][:, ki * 512:(ki + 1) * 512],
                            start=(kc == 0), stop=(kc == NKC - 1)), inc=(s_pvk, 1))
                    seg_pvk_end.append(s_pvk.n)

                pending_pv = None
                for (kc0, w_, kind) in segs:
                    sg = seg_ctr[0]
                    seg_ctr[0] += 1
                    pp = sg % 2
                    slot = sg % 3
                    if sg >= 2:
                        P.wait("tensor", s_pex, sg - 1)
                    for ki in range(w_):
                        kc = kc0 + ki
                        P.do("tensor", lambda e, buf=buf, kc=kc, qt_=qt_, pp=pp, ki=ki,
                             kind=kind: e.matmul(
                                 ps_sc[pp][:, ki * 512:(ki + 1) * 512],
                                 kt_sb[buf][0:DK, kc * 128:(kc + 1) * 128],
                                 qt_sb[buf][0:DK, qt_ * 512:(qt_ + 1) * 512],
                                 start=True, stop=(kind != 2)),
                             inc=(s_smm, 1) if kind != 2 else None)
                        if kind == 2:
                            bi = band_ctr[0]
                            band_ctr[0] += 1
                            band_g.append(sg)
                            bslot = bi % 2
                            if bi >= 2:
                                P.wait("sync", s_pex, band_g[bi - 2] + 1)
                            e0 = 3968 - 128 * kc + 512 * qt_
                            src = wbias_dram.copy()
                            src.offset = h * 8191 + e0
                            src.ap = bass_rust.VecI64Pair([[1, 128], [1, 512]])
                            if bi == 0:
                                P.wait("sync", s_wbw, wbw_all)
                            P.do("sync", lambda e, bslot=bslot, src=src: e.dma_start(
                                b3_sb[bslot][:], src), inc=(s_bnd[bslot], 16))
                            P.wait("tensor", s_bnd[bslot], s_bnd[bslot].n)
                            P.do("tensor", lambda e, pp=pp, ki=ki, bslot=bslot: e.matmul(
                                ps_sc[pp][:, ki * 512:(ki + 1) * 512], J_sb[:],
                                b3_sb[bslot][:], start=False, stop=True),
                                inc=(s_smm, 1))
                    cbcol = 3 * h + {0: 0, 1: 1, 2: 2}[kind]
                    P.wait("scalar", s_smm, s_smm.n)
                    if sg >= 3:
                        P.wait("scalar", s_pvk, seg_pvk_end[sg - 3])
                    P.do("scalar", lambda e, pp=pp, slot=slot, w_=w_, cbcol=cbcol: e.activation(
                        pt_sb[slot][:, 0:w_ * 512], ps_sc[pp][:, 0:w_ * 512], AF.Exp,
                        bias=cb[:, cbcol:cbcol + 1]), inc=(s_pex, 1))
                    pex_t = s_pex.n
                    if pending_pv is not None:
                        emit_pv(pending_pv)
                    pending_pv = (slot, kc0, w_, pex_t, buf)
                emit_pv(pending_pv)
                pv_i = s_pvk.n
                P.wait("vector", s_pvk, pv_i)
                P.do("vector", lambda e: e.tensor_scalar(rcp_s[64:65, 0:512], pso[64:65, :],
                                                         1e-37, None, op0=ALU.max))
                P.drain("vector")
                P.do("vector", lambda e: e.reciprocal(rcp_s[64:65, 512:1024],
                                                      rcp_s[64:65, 0:512]))
                P.drain("vector")
                P.do("vector", lambda e: e.tensor_copy(rcp_s[64:65, 0:512],
                                                       rcp_s[64:65, 512:1024]), inc=(s_on, 1))
                P.wait("tensor", s_on, s_on.n)
                P.wait("tensor", s_att, s_att.n)
                P.do("tensor", lambda e: e.matmul(
                    psb2[0:64, :], onesf[64:65, 0:64], rcp_s[64:65, 0:512],
                    start=True, stop=True), inc=(s_bc, 1))
                P.wait("vector", s_pvk, pv_i)
                P.wait("vector", s_att, s_att.n)
                P.do("vector", lambda e: e.tensor_copy(otmp[0:DK, :], pso[0:DK, :]),
                     inc=(s_oev, 1))
                P.wait("vector", s_bc, s_bc.n)
                P.wait("vector", s_oev, s_oev.n)
                rows = (h % 2) * 64
                cols = (h // 2) * S + qt_ * 512
                P.do("vector", lambda e, rows=rows, cols=cols: e.tensor_tensor(
                    attnT[rows:rows + 64, cols:cols + 512], otmp[0:DK, :], psb2[0:64, :],
                    op=ALU.mult), inc=(s_att, 1))
            P.do("tensor", lambda e: e.nop(), inc=(s_hfin, 1))
    att_all = s_att.n
    hfin_all = s_hfin.n
    if dump_cb is not None:
        dump_cb("attn", dict(attnT=attnT, s_att=s_att, cb=cb, sc=sc))
    if stage == "attn":
        return s_att

    s_wo = new_sem(nc, stk, pre + "wo")
    s_omm = new_sem(nc, stk, pre + "omm")
    s_xld = [new_sem(nc, stk, pre + f"xld{i}") for i in range(2)]
    s_sv = new_sem(nc, stk, pre + "sv")
    s_x1w = [new_sem(nc, stk, pre + f"x1w{i}") for i in range(2)]

    with _ES() as op_stk:
        _oc = op_stk.enter_context
        wo_sb = _oc(nc.sbuf_tensor(pre + "wo", [128, NDC * D], BF16))
        b2_bf = _oc(nc.sbuf_tensor(pre + "b2", [16, D], BF16))
        xld = [_oc(nc.sbuf_tensor(f"{pre}xld{i}", [128, D], F32)) for i in range(2)]
        seed = [_oc(nc.sbuf_tensor(f"{pre}seed{i}", [128, D], F32)) for i in range(2)]
        po = [_oc(nc.psum_tensor(f"{pre}po{i}", [128, D], F32)) for i in range(2)]
        P.wait("gpsimd", s_att, att_all)
        P.wait("sync", s_att, att_all)
        for dc in range(NDC):
            P.do("gpsimd", lambda e, dc=dc: e.dma_start(
                wo_sb[:, dc * D:(dc + 1) * D], w["wo"][dc * 128:(dc + 1) * 128, :]),
                inc=(s_wo, 16))
        P.do("gpsimd", lambda e: e.dma_start(b2_bf[:], w["b2"]), inc=(s_wo, 16))
        wo_all = s_wo.n
        P.wait("tensor", s_wo, wo_all)
        P.wait("tensor", s_gdone, gdone)
        for tc in range(NTC):
            bb = tc % 2
            if tc >= 2:
                P.wait("sync", s_sv, 2 * (tc - 1))
            P.do("sync", lambda e, bb=bb, tc=tc: e.dma_start(
                xld[bb][:], x_dram[tc * 128:(tc + 1) * 128, :]), inc=(s_xld[bb], 16))
            xld_i = s_xld[bb].n
            if tc >= 2:
                P.wait("tensor", s_sv, 2 * (tc - 1))
            for half in range(2):
                for ic in range(NDC):
                    P.do("tensor", lambda e, bb=bb, tc=tc, half=half, ic=ic: e.matmul(
                        po[bb][:, half * 512:(half + 1) * 512],
                        attnT[:, ic * S + tc * 128: ic * S + (tc + 1) * 128],
                        wo_sb[:, ic * D + half * 512: ic * D + (half + 1) * 512],
                        start=(ic == 0), stop=False))
                P.do("tensor", lambda e, bb=bb, tc=tc, half=half: e.matmul(
                    po[bb][:, half * 512:(half + 1) * 512], G_bf[0:16, tc * 128:(tc + 1) * 128],
                    b2_bf[0:16, half * 512:(half + 1) * 512], start=False, stop=True),
                    inc=(s_omm, 1))
            omm_i = s_omm.n
            P.wait("vector", s_omm, omm_i)
            P.wait("vector", s_xld[bb], xld_i)
            if tc >= 2:
                P.wait("vector", s_x1w[bb], ((tc - 2) // 2 + 1) * 16)
            for half in range(2):
                P.do("vector", lambda e, bb=bb, half=half: e.tensor_tensor(
                    seed[bb][:, half * 512:(half + 1) * 512],
                    po[bb][:, half * 512:(half + 1) * 512],
                    xld[bb][:, half * 512:(half + 1) * 512], op=ALU.add), inc=(s_sv, 1))
            P.wait("gpsimd", s_sv, s_sv.n)
            P.do("gpsimd", lambda e, bb=bb, tc=tc: e.dma_start(
                x1_dram[tc * 128:(tc + 1) * 128, :], seed[bb][:]), inc=(s_x1w[bb], 16))
        omm_all = s_omm.n
    if stage == "oproj":
        wait_all(P, "gpsimd", s_x1w)
        return s_x1w

    mid_stk.close()
    s_gat = new_sem(nc, stk, pre + "gat")
    s_etr = new_sem(nc, stk, pre + "etr")
    s_etv = new_sem(nc, stk, pre + "etv")
    s_ewq = [new_sem(nc, stk, pre + f"ewq{i}") for i in range(2)]
    s_eqf = new_sem(nc, stk, pre + "eqf")
    s_hmm = new_sem(nc, stk, pre + "hmm")
    s_gel = new_sem(nc, stk, pre + "gel")
    s_dmm = new_sem(nc, stk, pre + "dmm")
    s_stg = new_sem(nc, stk, pre + "stg")
    s_sca = new_sem(nc, stk, pre + "sca")

    with _ES() as ex_stk:
        _ec = ex_stk.enter_context
        ein_sb = _ec(nc.sbuf_tensor(pre + "ein", [128, 2 * D], BF16))
        einT = _ec(nc.sbuf_tensor(pre + "einT", [128, NDC * CAP], BF16))
        w1q_sb = [_ec(nc.sbuf_tensor(f"{pre}w1q{i}", [128, NDC * 1024], BF16)) for i in range(2)]
        w2q_sb = [_ec(nc.sbuf_tensor(f"{pre}w2q{i}", [128, 8 * D], BF16)) for i in range(2)]
        hT = _ec(nc.sbuf_tensor(pre + "hT", [128, 8 * CAP], BF16))
        b1T_sb = _ec(nc.sbuf_tensor(pre + "b1T", [128, 32 * E0], F32))
        scs = [_ec(nc.sbuf_tensor(f"{pre}scs{i}", [128, D], F32)) for i in range(2)]
        psh = [_ec(nc.psum_tensor(f"{pre}psh{i}", [128, 512], F32)) for i in range(2)]
        psd = [_ec(nc.psum_tensor(f"{pre}psd{i}", [128, D], F32)) for i in range(2)]
        pst = [_ec(nc.psum_tensor(f"{pre}pstr{i}", [128, 128], BF16)) for i in range(2)]

        def load_equarter(qi, e_, q_):
            buf = qi % 2
            if qi >= 2:
                P.wait("gpsimd", s_eqf, qi - 1)
            P.do("gpsimd", lambda e, e_=e_, q_=q_, buf=buf: e.dma_start(
                w1q_sb[buf][:].rearrange("p (dc f) -> p dc f", dc=NDC),
                w["w1"][e_].rearrange("(dc p) f -> p dc f", p=128)[:, :, q_ * 1024:(q_ + 1) * 1024]),
                inc=(s_ewq[buf], 16))
            P.do("gpsimd", lambda e, e_=e_, q_=q_, buf=buf: e.dma_start(
                w2q_sb[buf][:].rearrange("p (fc d) -> p fc d", fc=8),
                w["w2"][e_][q_ * 1024:(q_ + 1) * 1024, :].rearrange("(fc p) d -> p fc d", p=128)),
                inc=(s_ewq[buf], 16))

        def gather_expert(e_):
            if e_ >= 1:
                P.wait("gpsimd", s_etr, e_ * 16)
            for cc in range(2):
                P.do("gpsimd", lambda e, e_=e_, cc=cc: e.indirect_dma_start(
                    ein_sb[:, cc * D:(cc + 1) * D], None, xn_dram,
                    IOA(idxT[:, cc * E0 + e_:cc * E0 + e_ + 1], 0)), inc=(s_gat, 16))

        def transpose_expert(e_):
            gat_i = s_gat.n
            P.wait("tensor", s_gat, gat_i)
            if e_ >= 1:
                P.wait("tensor", s_hmm, e_ * 32)
            for i2 in range(16):
                cc, dc = i2 // NDC, i2 % NDC
                tb = i2 % 2
                if i2 >= 2:
                    P.wait("tensor", s_etv, s_etv.n - 1)
                P.do("tensor", lambda e, cc=cc, dc=dc, tb=tb: e.transpose(
                    pst[tb][:], ein_sb[:, cc * D + dc * 128: cc * D + (dc + 1) * 128],
                    idb_sb[:]), inc=(s_etr, 1))
                P.wait("vector", s_etr, s_etr.n)
                P.do("vector", lambda e, cc=cc, dc=dc, tb=tb: e.tensor_copy(
                    einT[:, dc * CAP + cc * 128: dc * CAP + (cc + 1) * 128], pst[tb][:]),
                    inc=(s_etv, 1))

        equarters = [(e_, q_) for e_ in range(E0) for q_ in range(4)]
        wait_all(P, "gpsimd", s_x1w)
        P.wait("gpsimd", s_gdone, gdone)
        P.do("gpsimd", lambda e: e.dma_start(b1T_sb[:], w["b1T"]), inc=(s_gat, 16))
        gather_expert(0)
        load_equarter(0, *equarters[0])
        load_equarter(1, *equarters[1])
        transpose_expert(0)
        for qi, (e_, q_) in enumerate(equarters):
            buf = qi % 2
            P.wait("tensor", s_ewq[buf], (qi // 2 + 1) * 32)
            if q_ == 0:
                P.wait("tensor", s_etv, s_etv.n)
            for fc in range(8):
                hb = fc % 2
                gidx = qi * 8 + fc
                if gidx >= 2:
                    P.wait("tensor", s_gel, gidx - 1)
                for dc in range(NDC):
                    P.do("tensor", lambda e, buf=buf, fc=fc, dc=dc, hb=hb: e.matmul(
                        psh[hb][:, 0:CAP],
                        w1q_sb[buf][:, dc * 1024 + fc * 128: dc * 1024 + (fc + 1) * 128],
                        einT[:, dc * CAP:(dc + 1) * CAP],
                        start=(dc == 0), stop=(dc == NDC - 1)),
                        inc=(s_hmm, 1) if dc == NDC - 1 else None)
                P.wait("scalar", s_hmm, s_hmm.n)
                if qi >= 1 and fc == 0:
                    P.wait("scalar", s_eqf, qi)
                P.do("scalar", lambda e, e_=e_, q_=q_, fc=fc, hb=hb: e.activation(
                    hT[:, fc * CAP:(fc + 1) * CAP], psh[hb][:, 0:CAP], AF.Gelu_apprx_tanh,
                    bias=b1T_sb[:, (q_ * 8 + fc) * E0 + e_:(q_ * 8 + fc) * E0 + e_ + 1]),
                    inc=(s_gel, 1))
            gel_i = s_gel.n
            P.wait("tensor", s_gel, gel_i)
            if q_ == 0 and e_ >= 1:
                P.wait("tensor", s_stg, e_ * 2)
            for cc in range(2):
                for fc in range(8):
                    for half in range(2):
                        last = (cc == 1 and fc == 7 and half == 1)
                        P.do("tensor", lambda e, buf=buf, cc=cc, fc=fc, half=half, q_=q_: e.matmul(
                            psd[cc][:, half * 512:(half + 1) * 512],
                            hT[:, fc * CAP + cc * 128: fc * CAP + (cc + 1) * 128],
                            w2q_sb[buf][:, fc * D + half * 512: fc * D + (half + 1) * 512],
                            start=(q_ == 0 and fc == 0), stop=(q_ == 3 and fc == 7)),
                            inc=(s_eqf, 1) if last else None)
            if qi + 2 < len(equarters):
                load_equarter(qi + 2, *equarters[qi + 2])
            if q_ == 3:
                P.wait("scalar", s_eqf, qi + 1)
                for cc in range(2):
                    if e_ >= 1 and cc == 0:
                        P.wait("scalar", s_sca, e_ * 32)
                    P.do("scalar", lambda e, e_=e_, cc=cc: e.activation(
                        scs[cc][:], psd[cc][:], AF.Copy,
                        scale=gatesT[:, cc * E0 + e_:cc * E0 + e_ + 1]), inc=(s_stg, 1))
                stg_i = s_stg.n
                P.wait("gpsimd", s_stg, stg_i)
                for cc in range(2):
                    P.do("gpsimd", lambda e, e_=e_, cc=cc: e.indirect_dma_start(
                        x1_dram, IOA(idxT[:, cc * E0 + e_:cc * E0 + e_ + 1], 0),
                        scs[cc][:], None, compute_op=ALU.add), inc=(s_sca, 16))
                if e_ + 1 < E0:
                    gather_expert(e_ + 1)
                    transpose_expert(e_ + 1)
    P.wait("gpsimd", s_sca, s_sca.n)
    l0t.close()
    return s_sca



from contextlib import ExitStack
from concourse.bass_utils import run_bass_kernel_spmd

_HC = None


def _hc():
    global _HC
    if _HC is None:
        _HC = host_consts()
    return _HC


def _build_l0_program():
    nc = bass.Bass("TRN2", target_bir_lowering=False)
    x_d = nc.dram_tensor("x", [S, D], F32, kind="ExternalInput").ap()
    x1_d = nc.dram_tensor("x1", [S, D], F32, kind="ExternalOutput").ap()
    w = {}
    for name, shape, dt in [
        ("wq", [D, D], F32), ("wk", [D, D], F32), ("wv", [D, D], F32), ("wo", [D, D], F32),
        ("table", [32, NH], F32), ("tabflat", [512], F32), ("oneh", [32, 8191], BF16),
        ("J", [128, 128], BF16), ("identf", [128, 128], F32), ("identb", [128, 128], BF16),
        ("gqT", [D, E0], F32), ("ltemp", [1], F32),
        ("w1", [E0, D, 4096], F32), ("b1T", [128, 32 * E0], F32),
        ("w2", [E0, 4096, D], F32), ("b2", [E0, D], F32),
    ]:
        w[name] = nc.dram_tensor(name, shape, dt, kind="ExternalInput").ap()
    with ExitStack() as stk:
        with nc.Block() as block:
            P = Prog(nc)
            build_l0(nc, P, stk, x_d, x1_d, w)
            P.finalize(block)
    return nc


def _build_fused_program():
    nc = bass.Bass("TRN2", target_bir_lowering=False)
    x_d = nc.dram_tensor("x", [S, D], F32, kind="ExternalInput").ap()
    out_d = nc.dram_tensor("out", [S, D], F32, kind="ExternalOutput").ap()
    x1_d = nc.dram_tensor("x1mid", [S, D], F32, kind="Internal").ap()
    w0 = {}
    for name, shape, dt in [
        ("wq", [D, D], F32), ("wk", [D, D], F32), ("wv", [D, D], F32), ("wo", [D, D], F32),
        ("table", [32, NH], F32), ("tabflat", [512], F32), ("oneh", [32, 8191], BF16),
        ("J", [128, 128], BF16), ("identf", [128, 128], F32), ("identb", [128, 128], BF16),
        ("gqT", [D, E0], F32), ("ltemp", [1], F32),
        ("w1", [E0, D, 4096], F32), ("b1T", [128, 32 * E0], F32),
        ("w2", [E0, 4096, D], F32), ("b2", [E0, D], F32),
    ]:
        w0[name] = nc.dram_tensor(name, shape, dt, kind="ExternalInput").ap()
    w1 = {
        "A2": nc.dram_tensor("A2", [128, D // 128], F32, kind="ExternalInput").ap(),
        "BT": nc.dram_tensor("BT", [D, D], F32, kind="ExternalInput").ap(),
        "WT": nc.dram_tensor("WT", [D, D], F32, kind="ExternalInput").ap(),
        "b": nc.dram_tensor("b", [D], F32, kind="ExternalInput").ap(),
        "rw": nc.dram_tensor("rw", [D, 8], F32, kind="ExternalInput").ap(),
        "up": nc.dram_tensor("up", [8, D, 4096], F32, kind="ExternalInput").ap(),
        "dn": nc.dram_tensor("dn", [8, 4096, D], F32, kind="ExternalInput").ap(),
    }
    with ExitStack() as stk:
        with nc.Block() as block:
            P = Prog(nc)
            s_sca = build_l0(nc, P, stk, x_d, x1_d, w0, pre="A_")
            for eng in Prog.ENGS:
                P.wait(eng, s_sca, s_sca.n)
            build_l1(nc, P, stk, x1_d, out_d, w1, pre="B_")
            P.finalize(block)
    return nc


def _build_l1_program():
    nc = bass.Bass("TRN2", target_bir_lowering=False)
    x1_d = nc.dram_tensor("x1", [S, D], F32, kind="ExternalInput").ap()
    out_d = nc.dram_tensor("out", [S, D], F32, kind="ExternalOutput").ap()
    w = {
        "A2": nc.dram_tensor("A2", [128, D // 128], F32, kind="ExternalInput").ap(),
        "BT": nc.dram_tensor("BT", [D, D], F32, kind="ExternalInput").ap(),
        "WT": nc.dram_tensor("WT", [D, D], F32, kind="ExternalInput").ap(),
        "b": nc.dram_tensor("b", [D], F32, kind="ExternalInput").ap(),
        "rw": nc.dram_tensor("rw", [D, 8], F32, kind="ExternalInput").ap(),
        "up": nc.dram_tensor("up", [8, D, 4096], F32, kind="ExternalInput").ap(),
        "dn": nc.dram_tensor("dn", [8, 4096, D], F32, kind="ExternalInput").ap(),
    }
    with ExitStack() as stk:
        with nc.Block() as block:
            P = Prog(nc)
            build_l1(nc, P, stk, x1_d, out_d, w)
            P.finalize(block)
    return nc


def kernel(
    L0_attn_hidden_states, L0_attn_w_q, L0_attn_w_k, L0_attn_w_v, L0_attn_w_o,
    L0_attn_rel_bias_table, L0_ffn_gate_query, L0_ffn_log_temp, L0_ffn_w1, L0_ffn_b1,
    L0_ffn_w2, L0_ffn_b2, L1_attn_A, L1_attn_B, L1_attn_W, L1_attn_b,
    L1_ffn_router_w, L1_ffn_up_w, L1_ffn_down_w,
):
    NCORES = 8
    hc = _hc()
    x_full = np.ascontiguousarray(np.asarray(L0_attn_hidden_states, np.float32))
    table = np.ascontiguousarray(np.asarray(L0_attn_rel_bias_table, np.float32))
    b1 = np.asarray(L0_ffn_b1, np.float32)
    b1T = np.ascontiguousarray(b1.T.reshape(32, 128, E0).transpose(1, 0, 2).reshape(128, 32 * E0))
    w_common0 = dict(
        wq=np.ascontiguousarray(np.asarray(L0_attn_w_q, np.float32)),
        wk=np.ascontiguousarray(np.asarray(L0_attn_w_k, np.float32)),
        wv=np.ascontiguousarray(np.asarray(L0_attn_w_v, np.float32)),
        wo=np.ascontiguousarray(np.asarray(L0_attn_w_o, np.float32)),
        table=table, tabflat=np.ascontiguousarray(table.flatten()),
        oneh=hc["oneh"], J=hc["J"], identf=hc["identf"], identb=hc["identb"],
        gqT=np.ascontiguousarray(np.asarray(L0_ffn_gate_query, np.float32).T),
        ltemp=np.asarray(L0_ffn_log_temp, np.float32).reshape(1),
        w1=np.ascontiguousarray(np.asarray(L0_ffn_w1, np.float32)), b1T=b1T,
        w2=np.ascontiguousarray(np.asarray(L0_ffn_w2, np.float32)),
        b2=np.ascontiguousarray(np.asarray(L0_ffn_b2, np.float32)),
    )
    w_common1 = dict(
        A2=np.ascontiguousarray(np.asarray(L1_attn_A, np.float32).reshape(-1, 128).T),
        BT=np.ascontiguousarray(np.asarray(L1_attn_B, np.float32).T),
        WT=np.ascontiguousarray(np.asarray(L1_attn_W, np.float32).T),
        b=np.ascontiguousarray(np.asarray(L1_attn_b, np.float32)),
        rw=np.ascontiguousarray(np.asarray(L1_ffn_router_w, np.float32)),
        up=np.ascontiguousarray(np.asarray(L1_ffn_up_w, np.float32)),
        dn=np.ascontiguousarray(np.asarray(L1_ffn_down_w, np.float32)),
    )
    nc_f = _build_fused_program()
    in_maps = [dict(x=np.ascontiguousarray(x_full[c]), **w_common0, **w_common1)
               for c in range(NCORES)]
    res = run_bass_kernel_spmd(nc_f, in_maps, core_ids=list(range(NCORES)))
    out = np.stack([np.asarray(r["out"]) for r in res.results], axis=0)
    return out.astype(np.float32)
